# version 14
# baseline (speedup 1.0000x reference)
import numpy as np
from contextlib import ExitStack

import concourse.bass as bass
import concourse.mybir as mybir
from concourse import dve_ops
from concourse.bass_utils import run_bass_kernel_spmd

F32 = mybir.dt.float32
BF16 = mybir.dt.bfloat16
AF = mybir.ActivationFunctionType
ALU = mybir.AluOpType
AX = mybir.AxisListType

D = 1024
DC = 8
H1 = 1408
HC = 11
NE = 8
NX = 9
G = 4
NG = 4
T = 2048
KV = 2560
NT = 4
NKT = 20
BIG = 30000.0
MASKVAL = -1.25e8

SL_MU, SL_RCP, SL_VAR, SL_RSTD = 0, 512, 1024, 1536
SL_C1, SL_C2, SL_RDL, SL_RDG = 0, 512, 1024, 1536
R_RW = 32
R_LG = 64

STAGE = 3


class Sem:
    __slots__ = ("h", "v", "name")

    def __init__(self, h, name):
        self.h, self.v, self.name = h, 0, name


class KB:
    ENGINES = ("tensor", "vector", "scalar", "gpsimd", "sync")

    def __init__(self, nc, es):
        self.nc, self.es = nc, es
        self.q = {e: [] for e in self.ENGINES}
        self.waited = {e: {} for e in self.ENGINES}

    def sem(self, name):
        return Sem(self.es.enter_context(self.nc.semaphore(name)), name)

    def op(self, eng, meth, *a, inc=None, inck=1, low_prec=None, **kw):
        if inc is not None:
            inc.v += inck
            tok = inc.v

            def f(E, meth=meth, a=a, kw=kw, s=inc.h, k=inck, lp=low_prec):
                if lp:
                    with self.nc.allow_low_precision(reason=lp):
                        getattr(E, meth)(*a, **kw).then_inc(s, k)
                else:
                    getattr(E, meth)(*a, **kw).then_inc(s, k)
        else:
            tok = None

            def f(E, meth=meth, a=a, kw=kw, lp=low_prec):
                if lp:
                    with self.nc.allow_low_precision(reason=lp):
                        getattr(E, meth)(*a, **kw)
                else:
                    getattr(E, meth)(*a, **kw)
        self.q[eng].append(f)
        return tok

    def wait(self, eng, sem, val=None):
        if val is None:
            val = sem.v
        if val <= 0 or self.waited[eng].get(sem, -1) >= val:
            return
        self.waited[eng][sem] = val
        self.q[eng].append(lambda E, s=sem.h, v=val: E.wait_ge(s, v))


def build(stage=STAGE):
    nc = bass.Bass("TRN2", target_bir_lowering=False)

    d_x = nc.dram_tensor("x_fm", [D, T], F32, kind="ExternalInput")
    d_wqkv = nc.dram_tensor("wqkv", [D, 3 * D], F32, kind="ExternalInput")
    d_wgate = nc.dram_tensor("wgate", [D, 1], F32, kind="ExternalInput")
    d_bgate = nc.dram_tensor("bgate", [1, 1], F32, kind="ExternalInput")
    d_wout = nc.dram_tensor("wout", [D, D], F32, kind="ExternalInput")
    d_bout = nc.dram_tensor("bout", [D], F32, kind="ExternalInput")
    d_gatewT = nc.dram_tensor("gatewT", [D, NE], F32, kind="ExternalInput")
    d_gateb = nc.dram_tensor("gateb", [1, NE], F32, kind="ExternalInput")
    d_w1 = nc.dram_tensor("w1", [NE, D, H1], F32, kind="ExternalInput")
    d_b1 = nc.dram_tensor("b1", [NE, H1], F32, kind="ExternalInput")
    d_w3 = nc.dram_tensor("w3", [NE, D, H1], F32, kind="ExternalInput")
    d_b3 = nc.dram_tensor("b3", [NE, H1], F32, kind="ExternalInput")
    d_w2 = nc.dram_tensor("w2", [NE, H1, D], F32, kind="ExternalInput")
    d_sw1 = nc.dram_tensor("sw1", [D, H1], F32, kind="ExternalInput")
    d_sb1 = nc.dram_tensor("sb1", [H1], F32, kind="ExternalInput")
    d_sw3 = nc.dram_tensor("sw3", [D, H1], F32, kind="ExternalInput")
    d_sb3 = nc.dram_tensor("sb3", [H1], F32, kind="ExternalInput")
    d_sw2 = nc.dram_tensor("sw2", [H1, D], F32, kind="ExternalInput")
    d_b2aug = nc.dram_tensor("b2aug", [NX, D], F32, kind="ExternalInput")
    d_mask = nc.dram_tensor("mask", [KV], F32, kind="ExternalInput")
    d_sel = nc.dram_tensor("sel", [128, 2], F32, kind="ExternalInput")
    d_y = nc.dram_tensor("y_fm", [D, T], F32, kind="ExternalOutput")
    d_snd = nc.dram_tensor("snd", [D, 576], BF16)
    d_rcv = nc.dram_tensor("rcv", [2 * D, 576], BF16)

    es = ExitStack()
    kb = KB(nc, es)

    st = kb.sem("st")
    sa = kb.sem("sa")
    sv = kb.sem("sv")
    sg = kb.sem("sg")
    scc = kb.sem("scc")
    gi = kb.sem("gi")
    gw = kb.sem("gw")
    gw2 = kb.sem("gw2")
    gq = kb.sem("gq")
    sxq = kb.sem("sxq")
    so = kb.sem("so")
    s_mu = kb.sem("s_mu")
    s_rs = kb.sem("s_rs")
    s_rw = [kb.sem("s_rw0"), kb.sem("s_rw1")]
    s_c1 = [kb.sem("s_c1a"), kb.sem("s_c1b")]
    s_c2 = [kb.sem("s_c2a"), kb.sem("s_c2b")]

    sbt = lambda name, shape, dt: es.enter_context(nc.sbuf_tensor(name, shape, dt))

    x = [sbt(f"x{c}", [128, T], F32) for c in range(DC)]
    xn = [sbt(f"xn{c}", [128, KV], BF16) for c in range(DC)]
    pb = [es.enter_context(nc.psum_tensor(f"pb{i}", [128, 512], F32)) for i in range(8)]

    rowpool = sbt("rowpool", [128, T], F32)
    masksb = sbt("masksb", [128, NKT], F32)
    bout_sb = sbt("boutsb", [128, DC], F32)
    gatewT_sb = sbt("gatewTsb", [128, DC * NE], BF16)
    gateb_sb = sbt("gatebsb", [128, NE], F32)
    wgate_sb = sbt("wgatesb", [128, DC], BF16)
    bgate_sb = sbt("bgatesb", [1, 1], F32)
    b2aug_sb = sbt("b2augsb", [NX, D], BF16)
    sb1_sb = sbt("sb1sb", [128, HC], F32)
    sb3_sb = sbt("sb3sb", [128, HC], F32)
    sel_sb = sbt("selsb", [128, 2], F32)
    ones_bf = sbt("onesbf", [128, 1], BF16)
    iv = sbt("iv", [128, NE], F32)
    ivB = sbt("ivB", [128, NE], F32)
    lgtok = sbt("lgtok", [128, 128], F32)
    rt_a = sbt("rt_a", [128, 128], F32)
    rt_b = sbt("rt_b", [128, 128], F32)
    rt_c = sbt("rt_c", [128, 128], F32)
    rt_m = sbt("rt_m", [128, 4], F32)
    rt_m2 = sbt("rt_m2", [128, 4], F32)
    rw_bf = sbt("rwbf", [32, 512], BF16)
    xb = [sbt("xb0", [128, 512], BF16)]
    xq = [sbt("xq0", [128, 512], BF16)]
    lnbf = sbt("lnbf", [1, 1024], BF16)
    ones128r = sbt("ones128r", [1, 128], BF16)
    ubuf = [sbt("ubuf0", [128, 512], F32)]
    tbuf = [sbt("tbuf0", [128, 512], F32)]
    rwb = [sbt("rwb0", [128, 512], F32)]

    TEN, VEC, ACT, GPS, SYN = "tensor", "vector", "scalar", "gpsimd", "sync"

    all_sems = [st, sa, sv, sg, scc, gi, gw, gw2, gq, sxq, s_mu, s_rs,
                s_rw[0], s_rw[1], s_c1[0], s_c1[1], s_c2[0], s_c2[1]]

    def join_all():
        for e in KB.ENGINES:
            for s in all_sems:
                kb.wait(e, s)

    pbwar = {i: [] for i in range(8)}

    def war(bank):
        for s, v in pbwar[bank]:
            kb.wait(TEN, s, v)
        pbwar[bank] = []

    def mm(out_ap, lhsT_ap, rhs_ap, start, stop, inc=None):
        return kb.op(TEN, "matmul", out_ap, lhsT_ap, rhs_ap,
                     start=start, stop=stop, inc=inc)

    def p0(c0, w=512):
        return rowpool[0:1, c0:c0 + w]

    def grow(c0=0, w=512):
        return rowpool[96:97, c0:c0 + w]

    for c in range(DC):
        kb.op(GPS, "dma_start", out=x[c][:], in_=d_x[128 * c:128 * (c + 1), :],
              inc=gi, inck=16)
    kb.op(GPS, "dma_start", out=masksb[:],
          in_=bass.AP(d_mask, 0, [[1, 128], [128, NKT]]),
          allow_slow_non_contiguous=True, inc=gi, inck=16)
    kb.op(GPS, "dma_start", out=bout_sb[:],
          in_=bass.AP(d_bout, 0, [[1, 128], [128, DC]]),
          allow_slow_non_contiguous=True, inc=gi, inck=16)
    kb.op(GPS, "dma_start",
          out=bass.AP(gatewT_sb, 0, [[DC * NE, 128], [NE, DC], [1, NE]]),
          in_=bass.AP(d_gatewT, 0, [[NE, 128], [128 * NE, DC], [1, NE]]),
          allow_slow_non_contiguous=True, inc=gi, inck=16)
    kb.op(GPS, "dma_start", out=gateb_sb[:],
          in_=bass.AP(d_gateb, 0, [[NE, 1], [0, 128], [1, NE]]), inc=gi, inck=16)
    kb.op(GPS, "dma_start", out=bass.AP(wgate_sb, 0, [[DC, 128], [1, DC]]),
          in_=bass.AP(d_wgate, 0, [[1, 128], [128, DC]]),
          allow_slow_non_contiguous=True, inc=gi, inck=16)
    kb.op(GPS, "dma_start", out=bgate_sb[:], in_=d_bgate[:], inc=gi, inck=16)
    kb.op(GPS, "dma_start", out=b2aug_sb[:], in_=d_b2aug[:], inc=gi, inck=16)
    kb.op(GPS, "dma_start", out=sb1_sb[:],
          in_=bass.AP(d_sb1, 0, [[1, 128], [128, HC]]),
          allow_slow_non_contiguous=True, inc=gi, inck=16)
    kb.op(GPS, "dma_start", out=sb3_sb[:],
          in_=bass.AP(d_sb3, 0, [[1, 128], [128, HC]]),
          allow_slow_non_contiguous=True, inc=gi, inck=16)
    kb.op(GPS, "dma_start", out=sel_sb[:], in_=d_sel[:], inc=gi, inck=16)
    gi_all = gi.v

    kb.op(GPS, "iota", iv[:], [[1, NE]], channel_multiplier=0,
          allow_small_or_imprecise_dtypes=True, inc=sg)
    kb.wait(GPS, sg, sg.v)
    kb.op(GPS, "tensor_scalar_add", ivB[:], iv[:], BIG, inc=sg)
    tok_ivB = sg.v
    kb.op(VEC, "memset", ones_bf[:], 1.0, inc=sv)
    kb.op(VEC, "memset", ones128r[:], 1.0, inc=sv)
    tok_ones = sv.v
    kb.op(VEC, "memset", bass.AP(rowpool, R_LG * T, [[T, 32], [1, 512]]), 0.0,
          inc=sv)
    tok_lgz = sv.v
    kb.op(VEC, "memset", rw_bf[:], 1.0, inc=sv)
    tok_rwones = sv.v
    kb.op(VEC, "memset", lgtok[:], 0.0, inc=sv)
    kb.op(VEC, "memset", rt_a[:], 0.0, inc=sv)
    kb.op(VEC, "memset", rt_b[:], 0.0, inc=sv)
    kb.op(VEC, "memset", rt_c[:], 0.0, inc=sv)
    tok_rtz = sv.v

    x_tok = {(c, t): (gi, 16 * DC) for c in range(DC) for t in range(NT)}
    xn_tok = {}
    ubuf_war = [None]
    tbuf_war = [None]
    xbq_war = [None]
    rwb_war = [None]
    lnbf_war = [None]
    lg_war = [None]
    rwbf_war = [None]

    def emit_ln(t):
        xc = slice(512 * t, 512 * t + 512)
        tok_pair = {}
        for c in range(DC):
            s_, v_ = x_tok[(c, t)]
            kb.wait(ACT, s_, v_)
            kb.wait(ACT, gi, gi_all)
            if c >= 1:
                kb.wait(ACT, st, tok_pair[c - 1])
            elif xbq_war[0] is not None:
                kb.wait(ACT, xbq_war[0][0], xbq_war[0][1])
            kb.op(ACT, "copy", xb[0][:], x[c][:, xc], inc=sa)
            kb.op(ACT, "square", xq[0][:], x[c][:, xc], inc=sa)
            kb.wait(TEN, sa, sa.v)
            kb.wait(VEC, sv, tok_ones)
            if c == 0:
                war(6)
                war(7)
            mm(pb[6][0:1, :], ones_bf[:], xb[0][:], c == 0, c == DC - 1)
            tok_pair[c] = kb.op(TEN, "matmul", pb[7][0:1, :], ones_bf[:],
                                xq[0][:], start=(c == 0),
                                stop=(c == DC - 1), inc=st)
        xbq_war[0] = (st, tok_pair[DC - 1])
        tok_stats = st.v
        kb.wait(VEC, st, tok_stats)
        kb.op(VEC, "tensor_scalar_mul", p0(SL_MU), pb[6][0:1, :], 1.0 / D, inc=sv)
        kb.wait(VEC, sv, sv.v)
        kb.op(VEC, "tensor_mul", p0(SL_RSTD), p0(SL_MU), p0(SL_MU), inc=sv)
        kb.wait(VEC, sv, sv.v)
        kb.op(VEC, "scalar_tensor_tensor", p0(SL_VAR), pb[7][0:1, :], 1.0 / D,
              p0(SL_RSTD), ALU.mult, ALU.subtract, inc=sv)
        tok_var = sv.v
        kb.wait(VEC, sv, tok_var)
        kb.op(VEC, "tensor_scalar_add", p0(SL_VAR), p0(SL_VAR), 1e-5, inc=sv)
        kb.wait(VEC, sv, sv.v)
        kb.op(VEC, "reciprocal", p0(SL_RSTD), p0(SL_VAR), inc=sv)
        tok_rcp = sv.v
        kb.wait(ACT, sv, tok_rcp)
        kb.op(ACT, "activation", p0(SL_RCP), p0(SL_RSTD), AF.Sqrt,
              bias=0.0, scale=1.0, inc=sa)
        kb.wait(ACT, sa, sa.v)
        if lnbf_war[0] is not None:
            kb.wait(ACT, lnbf_war[0][0], lnbf_war[0][1])
        kb.op(ACT, "copy", lnbf[:], p0(SL_MU, 1024), inc=sa)
        tok_lnbf = sa.v
        kb.wait(TEN, sa, tok_lnbf)
        mm(pb[6][:], ones128r[:], lnbf[0:1, 0:512], True, True)
        tkbc = mm(pb[7][:], ones128r[:], lnbf[0:1, 512:1024], True, True,
                  inc=st)
        lnbf_war[0] = (st, tkbc)
        for c in range(DC):
            s_, v_ = x_tok[(c, t)]
            kb.wait(VEC, s_, v_)
            kb.wait(VEC, st, tkbc)
            if tbuf_war[0] is not None:
                kb.wait(VEC, tbuf_war[0][0], tbuf_war[0][1])
            kb.wait(VEC, sv, sv.v)
            kb.op(VEC, "tensor_tensor", tbuf[0][:], x[c][:, xc], pb[6][:],
                  ALU.subtract, inc=sv)
            kb.wait(VEC, sv, sv.v)
            kb.wait(VEC, st, st.v)
            kb.wait(VEC, sxq, sxq.v)
            kb.op(VEC, "tensor_tensor", xn[c][:, 256 + 512 * t:256 + 512 * t + 512],
                  tbuf[0][:], pb[7][:], ALU.mult, inc=sv)
            xn_tok[(c, t)] = sv.v
            tbuf_war[0] = (sv, sv.v)
        pbwar[6].append((sv, sv.v))
        pbwar[7].append((sv, sv.v))
        return sv.v

    def emit_router(t):
        tc = 256 + 512 * t
        for c in range(DC):
            kb.wait(TEN, sv, xn_tok[(c, t)])
            if c == 0:
                war(5)
            mm(pb[5][0:NE, :], gatewT_sb[:, NE * c:NE * (c + 1)],
               xn[c][:, tc:tc + 512], c == 0, c == DC - 1,
               inc=st if c == DC - 1 else None)
        kb.wait(ACT, st, st.v)
        kb.wait(ACT, sv, tok_lgz)
        if lg_war[0] is not None:
            kb.wait(ACT, lg_war[0][0], lg_war[0][1])
        kb.op(ACT, "copy", bass.AP(rowpool, R_LG * T, [[T, NE], [1, 512]]),
              pb[5][0:NE, :], inc=sa)
        tok_lg = sa.v
        pbwar[5].append((sa, tok_lg))
        kb.wait(VEC, sa, tok_lg)
        kb.wait(VEC, sv, tok_rtz)
        for s_ in range(4):
            for J in range(4):
                kb.op(VEC, "transpose",
                      lgtok[J * 32:(J + 1) * 32, 32 * s_:32 * (s_ + 1)],
                      bass.AP(rowpool, R_LG * T + 128 * s_ + 32 * J,
                              [[T, 32], [1, 32]]),
                      inc=sv if (s_ == 3 and J == 3) else None)
        lg_war[0] = (sv, sv.v)
        kb.wait(VEC, sv, sv.v)
        v3 = lambda tns: bass.AP(tns, 0, [[128, 128], [32, 4], [1, NE]])
        sub = lambda tns, s_: tns[:, 32 * s_:32 * s_ + NE]

        def for4(meth, mk):
            for s_ in range(4):
                kb.op(VEC, meth, *mk(s_), inc=sv if s_ == 3 else None)
            kb.wait(VEC, sv, sv.v)

        kb.op(VEC, "tensor_reduce", rt_m[:], v3(lgtok), AX.X, ALU.max,
              negate=True, inc=sv)
        kb.wait(VEC, sv, sv.v)
        kb.wait(ACT, sv, sv.v)
        for s_ in range(4):
            kb.op(ACT, "activation", sub(rt_a, s_), sub(lgtok, s_), AF.Exp,
                  bias=rt_m[:, s_:s_ + 1], scale=1.0,
                  inc=sa if s_ == 3 else None)
        kb.wait(VEC, sa, sa.v)
        kb.op(VEC, "tensor_reduce", rt_m2[:], v3(rt_a), AX.X, ALU.add, inc=sv)
        kb.wait(VEC, sv, sv.v)
        kb.op(VEC, "reciprocal", rt_m[:], rt_m2[:], inc=sv)
        kb.wait(VEC, sv, sv.v)
        for4("tensor_scalar_mul",
             lambda s_: (sub(rt_a, s_), sub(rt_a, s_), rt_m[:, s_:s_ + 1]))
        for4("tensor_add",
             lambda s_: (sub(rt_b, s_), sub(rt_a, s_), gateb_sb[:]))
        kb.op(VEC, "tensor_reduce", rt_m[:], v3(rt_b), AX.X, ALU.max, inc=sv)
        kb.wait(VEC, sv, sv.v)
        kb.wait(VEC, sg, tok_ivB)
        for4("tensor_scalar",
             lambda s_: (sub(rt_c, s_), sub(rt_b, s_), rt_m[:, s_:s_ + 1], None,
                         ALU.is_ge))
        for4("scalar_tensor_tensor",
             lambda s_: (sub(rt_c, s_), sub(rt_c, s_), -BIG, ivB[:],
                         ALU.mult, ALU.add))
        kb.op(VEC, "tensor_reduce", rt_m[:], v3(rt_c), AX.X, ALU.min, inc=sv)
        kb.wait(VEC, sv, sv.v)
        for4("tensor_scalar",
             lambda s_: (sub(rt_c, s_), iv[:], rt_m[:, s_:s_ + 1], None,
                         ALU.is_equal))
        for4("scalar_tensor_tensor",
             lambda s_: (sub(rt_b, s_), sub(rt_c, s_), -BIG, sub(rt_b, s_),
                         ALU.mult, ALU.add))
        kb.op(VEC, "tensor_reduce", rt_m2[:], v3(rt_b), AX.X, ALU.max, inc=sv)
        kb.wait(VEC, sv, sv.v)
        for4("tensor_scalar",
             lambda s_: (sub(rt_b, s_), sub(rt_b, s_), rt_m2[:, s_:s_ + 1], None,
                         ALU.is_ge))
        for4("scalar_tensor_tensor",
             lambda s_: (sub(rt_b, s_), sub(rt_b, s_), -BIG, ivB[:],
                         ALU.mult, ALU.add))
        kb.op(VEC, "tensor_reduce", rt_m2[:], v3(rt_b), AX.X, ALU.min, inc=sv)
        kb.wait(VEC, sv, sv.v)
        for4("tensor_scalar",
             lambda s_: (sub(rt_b, s_), iv[:], rt_m2[:, s_:s_ + 1], None,
                         ALU.is_equal))
        for4("tensor_add",
             lambda s_: (sub(rt_b, s_), sub(rt_b, s_), sub(rt_c, s_)))
        for4("tensor_mul",
             lambda s_: (sub(rt_c, s_), sub(rt_a, s_), sub(rt_b, s_)))
        kb.wait(VEC, s_rw[0], s_rw[0].v)
        kb.wait(VEC, s_rw[1], s_rw[1].v)
        for s_ in range(4):
            for a_ in range(4):
                kb.op(VEC, "transpose",
                      bass.AP(rowpool, R_RW * T + 512 * t + 128 * s_ + 32 * a_,
                              [[T, 32], [1, 32]]),
                      rt_c[32 * a_:32 * (a_ + 1), 32 * s_:32 * (s_ + 1)],
                      inc=sv if (s_ == 3 and a_ == 3) else None)
        tok_rwT = sv.v
        kb.wait(ACT, sv, tok_rwT)
        if rwbf_war[0] is not None:
            kb.wait(ACT, rwbf_war[0][0], rwbf_war[0][1])
        kb.op(ACT, "copy", rw_bf[0:NE, 0:512],
              bass.AP(rowpool, R_RW * T + 512 * t, [[T, NE], [1, 512]]), inc=sa)
        return tok_rwT, sa.v

    pass_state = {"p": 0, "p1_last": {}, "p2_last": {},
                  "p1a_last": {}, "p1v_last": {}}

    def emit_ffn_pass():
        p = pass_state["p"]
        pass_state["p"] += 1
        p1_last = pass_state["p1_last"]
        p2_last = pass_state["p2_last"]
        p1a_last = pass_state["p1a_last"]
        p1v_last = pass_state["p1v_last"]
        join_all()
        fes = ExitStack()
        fsb = lambda name, shape, dt: fes.enter_context(
            nc.sbuf_tensor(f"{name}_p{p}", shape, dt))
        w1b = [fsb(f"w1b{k}", [128, H1], BF16) for k in range(DC)]
        w3b = [fsb(f"w3b{k}", [128, H1], BF16) for k in range(DC)]
        w2b = [fsb(f"w2b{k}", [128, D], BF16) for k in range(HC)]
        b1sb = fsb("b1sb", [128, HC], F32)
        b3sb = fsb("b3sb", [128, HC], F32)
        hb = [fsb(f"hb{k}", [128, 512], BF16) for k in range(HC)]
        rw_toks = {}

        def emit_A(t):
            emit_ln(t)
            rw_toks[t] = emit_router(t)

        def emit_bias(t):
            xc = slice(512 * t, 512 * t + 512)
            kb.wait(TEN, sa, rw_toks[t][1])
            kb.wait(TEN, sv, tok_rwones)
            last_bias_tok = None
            for mD in range(DC):
                war(mD)
                tk = mm(pb[mD][:], b2aug_sb[0:NX, 128 * mD:128 * (mD + 1)],
                        rw_bf[0:NX, 0:512], True, True, inc=st)
                last_bias_tok = tk
                kb.wait(VEC, st, tk)
                s_, v_ = x_tok[(mD, t)]
                kb.wait(VEC, s_, v_)
                kb.op(VEC, "scalar_tensor_tensor", x[mD][:, xc], pb[mD][:],
                      0.5, x[mD][:, xc], ALU.mult, ALU.add, inc=sv)
                x_tok[(mD, t)] = (sv, sv.v)
                pbwar[mD].append((sv, sv.v))
            rwbf_war[0] = (st, last_bias_tok)
        h_tok = {k: None for k in range(HC)}
        tok_p2_all = [None]
        for epos, e in enumerate(range(NX)):
            key = (p, epos)
            prev = (p, epos - 1) if epos > 0 else (p - 1, NX - 1)
            if prev in p1_last:
                kb.wait(GPS, st, p1_last[prev])
            if prev in p1a_last:
                kb.wait(GPS, sa, p1a_last[prev])
            if prev in p1v_last:
                kb.wait(GPS, sv, p1v_last[prev])
            for k in range(DC):
                src1 = d_w1[e, 128 * k:128 * (k + 1), :] if e < NE else \
                    d_sw1[128 * k:128 * (k + 1), :]
                src3 = d_w3[e, 128 * k:128 * (k + 1), :] if e < NE else \
                    d_sw3[128 * k:128 * (k + 1), :]
                kb.op(GPS, "dma_start", out=w1b[k][:], in_=src1, inc=gw, inck=16)
                kb.op(GPS, "dma_start", out=w3b[k][:], in_=src3, inc=gw, inck=16)
            if e < NE:
                kb.op(GPS, "dma_start", out=b1sb[:],
                      in_=bass.AP(d_b1, e * H1, [[1, 128], [128, HC]]),
                      allow_slow_non_contiguous=True, inc=gw, inck=16)
                kb.op(GPS, "dma_start", out=b3sb[:],
                      in_=bass.AP(d_b3, e * H1, [[1, 128], [128, HC]]),
                      allow_slow_non_contiguous=True, inc=gw, inck=16)
            gw_tok = gw.v
            if prev in p2_last:
                kb.wait(GPS, st, p2_last[prev])
            for k in range(HC):
                src2 = d_w2[e, 128 * k:128 * (k + 1), :] if e < NE else \
                    d_sw2[128 * k:128 * (k + 1), :]
                kb.op(GPS, "dma_start", out=w2b[k][:], in_=src2, inc=gw2, inck=16)
            gw2_tok = gw2.v
            bcol1 = (lambda m: b1sb[:, m:m + 1]) if e < NE else \
                (lambda m: sb1_sb[:, m:m + 1])
            bcol3 = (lambda m: b3sb[:, m:m + 1]) if e < NE else \
                (lambda m: sb3_sb[:, m:m + 1])
            for t in range(NT):
                if epos == 0 and t not in rw_toks:
                    emit_A(t)
                tc = 256 + 512 * t
                xc = slice(512 * t, 512 * t + 512)
                if e < NE:
                    if rwb_war[0] is not None:
                        kb.wait(SYN, rwb_war[0][0], rwb_war[0][1])
                    kb.wait(SYN, sv, rw_toks[t][0])
                    kb.op(SYN, "dma_start", out=rwb[0][:],
                          in_=bass.AP(rowpool, (R_RW + e) * T + 512 * t,
                                      [[T, 1], [0, 128], [1, 512]]),
                          inc=s_rw[0], inck=16)
                    rw_dma_tok = s_rw[0].v
                for m in range(HC):
                    bA, bB = m % 3, 3 + m % 3
                    war(bA)
                    war(bB)
                    kb.wait(TEN, gw, gw_tok)
                    for c in range(DC):
                        kb.wait(TEN, sv, xn_tok[(c, t)])
                    tokA = None
                    for k in range(DC):
                        tokA = mm(pb[bA][:], w1b[k][:, 128 * m:128 * (m + 1)],
                                  xn[k][:, tc:tc + 512], k == 0, k == DC - 1,
                                  inc=st if k == DC - 1 else None)
                    tokB = None
                    for k in range(DC):
                        tokB = mm(pb[bB][:], w3b[k][:, 128 * m:128 * (m + 1)],
                                  xn[k][:, tc:tc + 512], k == 0, k == DC - 1,
                                  inc=st if k == DC - 1 else None)
                    p1_last[key] = tokB
                    kb.wait(ACT, st, tokA)
                    kb.wait(ACT, gw, gw_tok)
                    if ubuf_war[0] is not None:
                        kb.wait(ACT, ubuf_war[0][0], ubuf_war[0][1])
                    kb.op(ACT, "activation", ubuf[0][:], pb[bA][:], AF.Identity,
                          bias=bcol1(m), scale=1.0, inc=sa)
                    tokU = sa.v
                    p1a_last[key] = tokU
                    pbwar[bA].append((sa, tokU))
                    kb.wait(VEC, st, tokB)
                    kb.wait(VEC, sa, tokU)
                    if tbuf_war[0] is not None:
                        kb.wait(VEC, tbuf_war[0][0], tbuf_war[0][1])
                    kb.op(VEC, "scalar_tensor_tensor", tbuf[0][:], pb[bB][:],
                          bcol3(m), ubuf[0][:], ALU.add, ALU.mult, inc=sv)
                    tokT = sv.v
                    p1v_last[key] = tokT
                    pbwar[bB].append((sv, tokT))
                    kb.wait(ACT, sv, tokT)
                    if tok_p2_all[0] is not None:
                        kb.wait(ACT, st, tok_p2_all[0])
                    if e == NE:
                        kb.op(ACT, "activation", hb[m][:], tbuf[0][:],
                              AF.Silu, inc=sa)
                        h_tok[m] = (sa, sa.v)
                        tbuf_war[0] = (sa, sa.v)
                        ubuf_war[0] = (sv, tokT)
                    else:
                        kb.op(ACT, "activation", ubuf[0][:], tbuf[0][:],
                              AF.Silu, inc=sa)
                        tokS = sa.v
                        tbuf_war[0] = (sa, tokS)
                        kb.wait(VEC, sa, tokS)
                        kb.wait(VEC, s_rw[0], rw_dma_tok)
                        if tok_p2_all[0] is not None:
                            kb.wait(VEC, st, tok_p2_all[0])
                        kb.op(VEC, "tensor_tensor", hb[m][:], ubuf[0][:],
                              rwb[0][:], ALU.mult, inc=sv)
                        h_tok[m] = (sv, sv.v)
                        ubuf_war[0] = (sv, sv.v)
                if e < NE:
                    rwb_war[0] = h_tok[HC - 1]
                if epos == 0:
                    emit_bias(t)
                tok_y = {}
                for k in range(HC):
                    kb.wait(TEN, gw2, gw2_tok)
                    kb.wait(TEN, h_tok[k][0], h_tok[k][1])
                    for mD in range(DC):
                        if k == 0:
                            war(mD)
                        tk = mm(pb[mD][:], w2b[k][:, 128 * mD:128 * (mD + 1)],
                                hb[k][:], k == 0, k == HC - 1,
                                inc=st if k == HC - 1 else None)
                        if k == HC - 1:
                            tok_y[mD] = tk
                p2_last[key] = tok_y[DC - 1]
                tok_p2_all[0] = tok_y[DC - 1]
                for mD in range(DC):
                    kb.wait(VEC, st, tok_y[mD])
                    s_, v_ = x_tok[(mD, t)]
                    kb.wait(VEC, s_, v_)
                    kb.op(VEC, "scalar_tensor_tensor", x[mD][:, xc], pb[mD][:],
                          0.5, x[mD][:, xc], ALU.mult, ALU.add, inc=sv)
                    x_tok[(mD, t)] = (sv, sv.v)
                    pbwar[mD].append((sv, sv.v))
        join_all()
        fes.close()

    def emit_attention():
        join_all()
        aes = ExitStack()
        asb = lambda name, shape, dt: aes.enter_context(
            nc.sbuf_tensor(name, shape, dt))
        wq_s = [asb(f"wqs{c}", [128, 256], BF16) for c in range(DC)]
        wk_s = [asb(f"wks{c}", [128, 256], BF16) for c in range(DC)]
        wv_s = [asb(f"wvs{c}", [128, 256], BF16) for c in range(DC)]
        wout_g = [asb(f"woutg{j}", [128, D], BF16) for j in range(2)]
        q_sb = [asb(f"qsb{j}", [128, T], BF16) for j in range(2)]
        k_sb = [asb(f"ksb{j}", [128, KV], BF16) for j in range(2)]
        v_sb = asb("vsb", [128, NKT * 260], BF16)
        vstr = asb("vstr", [128, 272], BF16)
        kstr = asb("kstr", [128, 256], BF16)
        xnp = [asb(f"xnp{c}", [128, 64], BF16) for c in range(DC)]
        expS = asb("expS", [128, 8 * 512], BF16)
        expSg = [asb("expSg0", [128, 512], BF16)]
        rcpbf = asb("rcpbf", [1, 1024], BF16)
        ones64 = asb("ones64", [1, 64], BF16)
        gbb1m = asb("gbb1m", [64, 512], F32)
        cbuf1 = asb("cbuf1", [64, 512], F32)
        cbuf2 = asb("cbuf2", [64, 512], F32)
        gbb = asb("gbb", [64, 512], F32)
        comb = [asb(f"comb{j}", [128, T], BF16) for j in range(2)]
        exA = asb("exA", [128, 576], BF16)
        exB = asb("exB", [128, 576], BF16)
        xstr = exB
        for t in range(NT):
            emit_ln(t)
        for t in range(NT):
            tc = 256 + 512 * t
            war(4)
            for c in range(DC):
                kb.wait(TEN, sv, xn_tok[(c, t)])
                mm(pb[4][0:1, :], wgate_sb[:, c:c + 1], xn[c][:, tc:tc + 512],
                   c == 0, c == DC - 1, inc=st if c == DC - 1 else None)
            kb.wait(ACT, st, st.v)
            kb.op(ACT, "activation", grow(512 * t), pb[4][0:1, :],
                  AF.Sigmoid, bias=bgate_sb[0:1, 0:1], scale=1.0, inc=sa)
            pbwar[4].append((sa, sa.v))
        tok_g = sa.v

        qkv_last = {}
        oproj_last = {}
        gq_toks = {}

        def emit_group_loads(g):
            if g - 1 in qkv_last:
                kb.wait(GPS, st, qkv_last[g - 1])
            for c in range(DC):
                kb.op(GPS, "dma_start", out=wq_s[c][:],
                      in_=d_wqkv[128 * c:128 * (c + 1), 256 * g:256 * (g + 1)],
                      inc=gq, inck=16)
                kb.op(GPS, "dma_start", out=wk_s[c][:],
                      in_=d_wqkv[128 * c:128 * (c + 1),
                                 D + 256 * g:D + 256 * (g + 1)],
                      inc=gq, inck=16)
                kb.op(GPS, "dma_start", out=wv_s[c][:],
                      in_=d_wqkv[128 * c:128 * (c + 1),
                                 2 * D + 256 * g:2 * D + 256 * (g + 1)],
                      inc=gq, inck=16)
            if g - 1 in oproj_last:
                kb.wait(GPS, st, oproj_last[g - 1])
            for j in range(2):
                kb.op(GPS, "dma_start", out=wout_g[j][:],
                      in_=d_wout[256 * g + 128 * j:256 * g + 128 * (j + 1), :],
                      inc=gq, inck=16)
            gq_toks[g] = gq.v

        kb.op(VEC, "memset", ones64[:], 1.0, inc=sv)
        tok_ones64 = sv.v
        emit_group_loads(0)

        for c in range(DC):
            for t in range(NT):
                kb.wait(SYN, sv, xn_tok[(c, t)])
        for c in range(DC):
            for t in range(NT):
                kb.wait(VEC, sv, xn_tok[(c, t)])
            kb.op(VEC, "tensor_copy", xstr[:, 64 * c:64 * (c + 1)],
                  bass.AP(xn[c], 256, [[KV, 128], [32, 64]]), inc=sv)
        tok_xstr = sv.v
        kb.wait(SYN, sv, tok_xstr)
        for c in range(DC):
            kb.op(SYN, "dma_start", out=d_snd[128 * c:128 * (c + 1), 0:256],
                  in_=xn[c][:, 256:512], inc=sxq, inck=16)
            kb.op(SYN, "dma_start", out=d_snd[128 * c:128 * (c + 1), 256:512],
                  in_=xn[c][:, 2048:2304], inc=sxq, inck=16)
            kb.op(SYN, "dma_start", out=d_snd[128 * c:128 * (c + 1), 512:576],
                  in_=xstr[:, 64 * c:64 * (c + 1)], inc=sxq, inck=16)
        kb.wait(GPS, sxq, sxq.v)
        kb.op(GPS, "collective_compute", "AllGather", ALU.bypass,
              replica_groups=[[0, 1], [2, 3], [4, 5], [6, 7]],
              ins=[d_snd[:]], outs=[d_rcv[:]], inc=scc)
        kb.wait(SYN, scc, scc.v)
        for c in range(DC):
            kb.op(SYN, "dma_start", out=exA[:],
                  in_=d_rcv[128 * c:128 * (c + 1), :], inc=sxq, inck=16)
            kb.op(SYN, "dma_start", out=exB[:],
                  in_=d_rcv[D + 128 * c:D + 128 * (c + 1), :], inc=sxq, inck=16)
            kb.wait(VEC, sxq, sxq.v)
            kb.wait(VEC, gi, gi_all)
            kb.op(VEC, "tensor_scalar_mul", exA[:], exA[:], sel_sb[:, 0:1],
                  inc=sv)
            kb.wait(VEC, sv, sv.v)
            kb.op(VEC, "scalar_tensor_tensor", exA[:], exB[:], sel_sb[:, 1:2],
                  exA[:], ALU.mult, ALU.add, inc=sv)
            tok_m = sv.v
            kb.wait(SYN, sv, tok_m)
            kb.op(SYN, "dma_start", out=xn[c][:, 0:256], in_=exA[:, 256:512],
                  inc=sxq, inck=16)
            kb.op(SYN, "dma_start", out=xn[c][:, 2304:2560], in_=exA[:, 0:256],
                  inc=sxq, inck=16)
            kb.op(SYN, "dma_start", out=xnp[c][:], in_=exA[:, 512:576],
                  inc=sxq, inck=16)
            kb.wait(SYN, sxq, sxq.v)
        tok_halo = sxq.v

        comb_war = {0: [], 1: []}
        qk_war = []
        expS_war = {}
        expSg_war = [None]
        prev_opl = [None]
        gbb_war = [None]
        gbb1m_tok = [0]
        rcpbf_war = [None]
        rcpslot_war = [None]
        for g in range(NG):
            if g > 0:
                emit_group_loads(g)
            gq_tok = gq_toks[g]

            rot_i = [0]

            def acc_evac(lhs_fn, rhs_fn, out_fn, n_acc, width=512, m_rows=128):
                bank = rot_i[0] % 4
                rot_i[0] += 1
                war(bank)
                tk = None
                for c in range(n_acc):
                    tk = mm(pb[bank][0:m_rows, 0:width], lhs_fn(c), rhs_fn(c),
                            c == 0, c == n_acc - 1,
                            inc=st if c == n_acc - 1 else None)
                kb.wait(ACT, st, tk)
                kb.op(ACT, "copy", out_fn(), pb[bank][0:m_rows, 0:width], inc=sa)
                pbwar[bank].append((sa, sa.v))
                return tk, sa.v

            kb.wait(TEN, gq, gq_tok)
            for s_, v_ in qk_war:
                kb.wait(ACT, s_, v_)
            qk_war = []
            last_qkv = None
            for mq in range(2):
                for t in range(NT):
                    for c in range(DC):
                        kb.wait(TEN, sv, xn_tok[(c, t)])
                    tmm, _ = acc_evac(
                        lambda c, mq=mq: wq_s[c][:, 128 * mq:128 * (mq + 1)],
                        lambda c, t=t: xn[c][:, 256 + 512 * t:256 + 512 * t + 512],
                        lambda mq=mq, t=t: q_sb[mq][:, 512 * t:512 * t + 512], DC)
                    last_qkv = tmm

            def k_part(mk, kt):
                return acc_evac(
                    lambda c, mk=mk: wk_s[c][:, 128 * mk:128 * (mk + 1)],
                    lambda c, kt=kt: xn[c][:, 512 * kt:512 * kt + 512],
                    lambda mk=mk, kt=kt: k_sb[mk][:, 512 * kt:512 * kt + 512],
                    DC)

            def v_part(vt):
                tmm, tev = acc_evac(
                    lambda c, vt=vt: xn[c][:, 128 * vt:128 * (vt + 1)],
                    lambda c: wv_s[c][:],
                    lambda vt=vt: bass.AP(v_sb, 260 * vt,
                                          [[NKT * 260, 128], [65, 4], [1, 64]]),
                    DC, width=256)
                kb.wait(VEC, sa, tev)
                if prev_opl[0] is not None:
                    kb.wait(VEC, st, prev_opl[0])
                kb.op(VEC, "memset",
                      bass.AP(v_sb, 260 * vt + 64,
                              [[NKT * 260, 128], [65, 4], [1, 1]]), 1.0, inc=sv)
                return tmm, (sv, sv.v)

            v_ev = []
            for mk in range(2):
                for kt in (1, 2, 3):
                    tmm, tev = k_part(mk, kt)
                    last_qkv = tmm
            for vt in range(2, 18):
                tmm, ev = v_part(vt)
                last_qkv = tmm
                v_ev.append(ev)
            tmm, vstr_o_tok = acc_evac(
                lambda c: bass.AP(xn[c], 256, [[KV, 128], [32, 64]]),
                lambda c: wv_s[c][:],
                lambda: bass.AP(vstr, 0, [[272, 64], [65, 4], [1, 64]]),
                DC, width=256, m_rows=64)
            last_qkv = tmm
            kb.wait(TEN, sxq, tok_halo)
            k_ev_tok = None
            for mk in range(2):
                for kt in (0, 4):
                    tmm, tev = k_part(mk, kt)
                    last_qkv = tmm
                    k_ev_tok = tev
            for vt in (0, 1, 18, 19):
                tmm, ev = v_part(vt)
                last_qkv = tmm
                v_ev.append(ev)
            kstr_p_tok = None
            for mk in range(2):
                tmm, tev = acc_evac(
                    lambda c, mk=mk: wk_s[c][:, 128 * mk:128 * (mk + 1)],
                    lambda c: xnp[c][:],
                    lambda mk=mk: kstr[:, 128 * mk + 64:128 * mk + 128],
                    DC, width=64)
                last_qkv = tmm
                kstr_p_tok = tev
            tmm, vstr_p_tok = acc_evac(
                lambda c: xnp[c][:],
                lambda c: wv_s[c][:],
                lambda: bass.AP(vstr, 64 * 272, [[272, 64], [65, 4], [1, 64]]),
                DC, width=256, m_rows=64)
            last_qkv = tmm
            kb.wait(VEC, sa, vstr_p_tok)
            kb.wait(VEC, sa, vstr_o_tok)
            if prev_opl[0] is not None:
                kb.wait(VEC, st, prev_opl[0])
            kb.op(VEC, "memset",
                  bass.AP(vstr, 64, [[272, 128], [65, 4], [1, 1]]),
                  1.0, inc=sv)
            vstr_ones_tok = sv.v
            qkv_last[g] = last_qkv
            if prev_opl[0] is not None:
                kb.wait(VEC, st, prev_opl[0])
            kb.wait(VEC, sa, k_ev_tok)
            for ch in range(2):
                kb.op(VEC, "tensor_copy", kstr[:, 128 * ch:128 * ch + 64],
                      bass.AP(k_sb[ch], 256, [[KV, 128], [32, 64]]), inc=sv)
            str_tok = sv.v

            combine_last = None
            pend = None
            pend2 = None

            def emit_pv(P):
                war(4 + P["par"])
                kb.wait(TEN, sa, P["tokE"][DC - 1])
                tkPV = None
                for kc in range(DC):
                    tkPV = mm(pb[4 + P["par"]][0:65, :],
                              v_sb[:, 260 * (4 * P["w"] + kc) + 65 * P["j"]:
                                   260 * (4 * P["w"] + kc) + 65 * (P["j"] + 1)],
                              expS[:, 512 * kc:512 * (kc + 1)],
                              kc == 0, kc == DC - 1, inc=st)
                    expS_war[kc] = (st, tkPV)
                kb.wait(TEN, sa, P["tokEg"])
                kb.wait(TEN, sv, vstr_ones_tok)
                kb.wait(TEN, sa, vstr_p_tok)
                kb.wait(TEN, sa, vstr_o_tok)
                gbank = 6 + P["par"]
                war(gbank)
                tkPVg = mm(pb[gbank][0:65, :],
                           vstr[:, 65 * P["j"]:65 * (P["j"] + 1)],
                           expSg[0][:], True, True, inc=st)
                expSg_war[0] = (st, tkPVg)
                P["tkPV"], P["tkPVg"] = tkPV, tkPVg

            def emit_combine_a(P):
                par, j, w = P["par"], P["j"], P["w"]
                if j == 0:
                    if gbb_war[0] is not None:
                        kb.wait(SYN, gbb_war[0][0], gbb_war[0][1])
                    kb.wait(SYN, sa, tok_g)
                    kb.op(SYN, "dma_start", out=gbb[:],
                          in_=bass.AP(rowpool, 96 * T + 512 * w,
                                      [[T, 1], [0, 64], [1, 512]]),
                          inc=s_c1[1], inck=16)
                    kb.wait(VEC, s_c1[1], s_c1[1].v)
                    kb.op(VEC, "tensor_scalar", gbb1m[:], gbb[:], -1.0, 1.0,
                          ALU.mult, ALU.add, inc=sv)
                    gbb1m_tok[0] = sv.v
                kb.wait(VEC, st, P["tkPV"])
                if rcpslot_war[0] is not None:
                    kb.wait(VEC, rcpslot_war[0][0], rcpslot_war[0][1])
                kb.op(VEC, "reciprocal", p0(SL_C1), pb[4 + par][64:65, :],
                      inc=sv)
                pbwar[4 + par].append((sv, sv.v))
                kb.wait(VEC, st, P["tkPVg"])
                kb.op(VEC, "reciprocal", p0(SL_C2), pb[6 + par][64:65, :],
                      inc=sv)
                pbwar[6 + par].append((sv, sv.v))
                tok_rcp = sv.v
                kb.wait(ACT, sv, tok_rcp)
                if rcpbf_war[0] is not None:
                    kb.wait(ACT, rcpbf_war[0][0], rcpbf_war[0][1])
                kb.op(ACT, "copy", rcpbf[:], p0(SL_C1, 1024), inc=sa)
                P["tok_rbf"] = sa.v
                rcpslot_war[0] = (sa, sa.v)

            def emit_combine(P):
                par, j, w, ch, ro = P["par"], P["j"], P["w"], P["ch"], P["ro"]
                bb = 0
                kb.wait(TEN, sa, P["tok_rbf"])
                kb.wait(TEN, sv, tok_ones64)
                war(bb)
                mm(pb[bb][0:64, :], ones64[:], rcpbf[0:1, 0:512], True, True)
                tkB = mm(pb[bb][64:128, :], ones64[:], rcpbf[0:1, 512:1024],
                         True, True, inc=st)
                rcpbf_war[0] = (st, tkB)
                kb.wait(VEC, st, tkB)
                kb.wait(VEC, sv, gbb1m_tok[0])
                kb.wait(VEC, sv, sv.v)
                kb.op(VEC, "tensor_tensor", cbuf1[:], gbb[:], pb[bb][0:64, :],
                      ALU.mult, inc=sv)
                kb.wait(VEC, sv, sv.v)
                kb.op(VEC, "tensor_tensor", cbuf2[:], gbb1m[:],
                      pb[bb][64:128, :], ALU.mult, inc=sv)
                gbb_war[0] = (sv, sv.v)
                pbwar[bb].append((sv, sv.v))
                kb.wait(VEC, st, P["tkPV"])
                kb.wait(VEC, st, P["tkPVg"])
                kb.wait(VEC, sv, sv.v)
                kb.op(VEC, "tensor_tensor", cbuf1[:], pb[4 + par][0:64, :],
                      cbuf1[:], ALU.mult, inc=sv)
                pbwar[4 + par].append((sv, sv.v))
                kb.wait(VEC, sv, sv.v)
                kb.op(VEC, "tensor_tensor", cbuf2[:], pb[6 + par][0:64, :],
                      cbuf2[:], ALU.mult, inc=sv)
                pbwar[6 + par].append((sv, sv.v))
                kb.wait(VEC, sv, sv.v)
                for s_, v_ in comb_war[ch]:
                    kb.wait(VEC, s_, v_)
                comb_war[ch] = []
                kb.op(VEC, "tensor_add",
                      comb[ch][ro:ro + 64, 512 * w:512 * (w + 1)],
                      cbuf1[:], cbuf2[:], inc=sv)
                return sv.v

            def emit_s_half(i, kcs, tokS):
                w, j = i // 4, i % 4
                ch, ro = j // 2, 64 * (j % 2)
                for kc in kcs:
                    bank = kc % 4
                    war(bank)
                    kb.wait(TEN, sa, k_ev_tok)
                    tokS[kc] = mm(
                        pb[bank][:],
                        k_sb[ch][ro:ro + 64,
                                 512 * w + 128 * kc:512 * w + 128 * (kc + 1)],
                        q_sb[ch][ro:ro + 64, 512 * w:512 * (w + 1)],
                        True, True, inc=st)

            def emit_exp_half(i, kcs, tokS, tokE):
                w = i // 4
                if (DC - 1) in expS_war:
                    kb.wait(ACT, expS_war[DC - 1][0], expS_war[DC - 1][1])
                for kc in kcs:
                    bank = kc % 4
                    kb.wait(ACT, st, tokS[kc])
                    kb.op(ACT, "activation", expS[:, 512 * kc:512 * (kc + 1)],
                          pb[bank][:], AF.Exp,
                          bias=masksb[:, 4 * w + kc:4 * w + kc + 1], scale=0.125,
                          inc=sa)
                    tokE[kc] = sa.v
                    pbwar[bank].append((sa, sa.v))

            for i in range(16):
                w, j = i // 4, i % 4
                ch, ro = j // 2, 64 * (j % 2)
                par = i % 2
                tokS, tokE = {}, {}
                if pend2 is not None:
                    emit_combine_a(pend2)
                emit_s_half(i, (0, 1, 2, 3), tokS)
                if pend is not None:
                    emit_pv(pend)
                emit_exp_half(i, (0, 1, 2, 3), tokS, tokE)
                emit_s_half(i, (4, 5, 6, 7), tokS)
                emit_exp_half(i, (4, 5, 6, 7), tokS, tokE)
                war(0)
                kb.wait(TEN, sv, str_tok)
                kb.wait(TEN, sa, kstr_p_tok)
                tkSg = mm(pb[0][:], kstr[ro:ro + 64, 128 * ch:128 * (ch + 1)],
                          q_sb[ch][ro:ro + 64, 512 * w:512 * (w + 1)],
                          True, True, inc=st)
                kb.wait(ACT, st, tkSg)
                if expSg_war[0] is not None:
                    kb.wait(ACT, expSg_war[0][0], expSg_war[0][1])
                kb.op(ACT, "activation", expSg[0][:], pb[0][:], AF.Exp,
                      bias=0.0, scale=0.125, inc=sa)
                tokEg = sa.v
                pbwar[0].append((sa, tokEg))
                if pend2 is not None:
                    combine_last = emit_combine(pend2)
                pend2 = pend
                pend = {"tokE": tokE, "tokEg": tokEg, "par": par, "j": j,
                        "w": w, "ch": ch, "ro": ro}
            emit_pv(pend)
            emit_combine_a(pend2)
            combine_last = emit_combine(pend2)
            emit_combine_a(pend)
            combine_last = emit_combine(pend)
            pend = None
            pend2 = None

            kb.wait(TEN, sv, combine_last)
            kb.wait(TEN, gq, gq_tok)
            opl = None
            for t in range(NT):
                xc = slice(512 * t, 512 * t + 512)
                for mD in range(DC):
                    bank = (t * DC + mD) % 4
                    war(bank)
                    mm(pb[bank][:], wout_g[0][:, 128 * mD:128 * (mD + 1)],
                       comb[0][:, 512 * t:512 * t + 512], True, False)
                    opl = mm(pb[bank][:], wout_g[1][:, 128 * mD:128 * (mD + 1)],
                             comb[1][:, 512 * t:512 * t + 512], False, True,
                             inc=st)
                    kb.wait(VEC, st, opl)
                    s_, v_ = x_tok[(mD, t)]
                    kb.wait(VEC, s_, v_)
                    scl = bout_sb[:, mD:mD + 1] if g == 0 else 0.0
                    kb.op(VEC, "scalar_tensor_tensor", x[mD][:, xc],
                          pb[bank][:], scl, x[mD][:, xc], ALU.add, ALU.add,
                          inc=sv)
                    x_tok[(mD, t)] = (sv, sv.v)
                    pbwar[bank].append((sv, sv.v))
            oproj_last[g] = opl
            prev_opl[0] = opl
            comb_war[0] = [(st, opl)]
            comb_war[1] = [(st, opl)]
            qk_war = [(st, opl)]
        join_all()
        aes.close()

    if stage >= 1:
        emit_ffn_pass()
    if stage >= 2:
        emit_attention()
    if stage >= 3:
        emit_ffn_pass()

    for c in range(DC):
        for t in range(NT):
            s_, v_ = x_tok[(c, t)]
            kb.wait(SYN, s_, v_)
    for c in range(DC):
        kb.op(SYN, "dma_start", out=d_y[128 * c:128 * (c + 1), :], in_=x[c][:],
              inc=so, inck=16)
    kb.wait(SYN, so, so.v)

    with es, nc.allow_non_contiguous_dma(reason="small one-time gathers (strided halo/kv index patterns)"), nc.Block() as block:
        @block.tensor
        def _(E):
            for f in kb.q["tensor"]:
                f(E)

        @block.vector
        def _(E):
            for f in kb.q["vector"]:
                f(E)

        @block.scalar
        def _(E):
            for f in kb.q["scalar"]:
                f(E)

        @block.gpsimd
        def _(E):
            for f in kb.q["gpsimd"]:
                f(E)

        @block.sync
        def _(E):
            for f in kb.q["sync"]:
                f(E)

    return nc


def make_in_maps(inputs):
    f32 = lambda a: np.ascontiguousarray(np.asarray(a), dtype=np.float32)
    xin = f32(inputs["L0_attn_x"])
    shared = {
        "wqkv": f32(inputs["L0_attn_W_qkv"]),
        "wgate": f32(inputs["L0_attn_W_gate"]),
        "bgate": f32(inputs["L0_attn_b_gate"]).reshape(1, 1),
        "wout": f32(inputs["L0_attn_W_out"]),
        "bout": f32(inputs["L0_attn_b_out"]),
        "gatewT": np.ascontiguousarray(f32(inputs["L0_ffn_gate_w"]).T),
        "gateb": f32(inputs["L0_ffn_gate_b"]).reshape(1, NE),
        "w1": f32(inputs["L0_ffn_w1"]),
        "b1": f32(inputs["L0_ffn_b1"]),
        "w3": f32(inputs["L0_ffn_w3"]),
        "b3": f32(inputs["L0_ffn_b3"]),
        "w2": f32(inputs["L0_ffn_w2"]),
        "sw1": f32(inputs["L0_ffn_sw1"]),
        "sb1": f32(inputs["L0_ffn_sb1"]),
        "sw3": f32(inputs["L0_ffn_sw3"]),
        "sb3": f32(inputs["L0_ffn_sb3"]),
        "sw2": f32(inputs["L0_ffn_sw2"]),
        "b2aug": np.ascontiguousarray(
            np.concatenate([f32(inputs["L0_ffn_b2"]),
                            f32(inputs["L0_ffn_sb2"])[None, :]], axis=0)),
    }
    in_maps = []
    for i in range(8):
        b, h = i // 2, i % 2
        xs = np.ascontiguousarray(xin[b, 2048 * h:2048 * (h + 1), :].T)
        mask = np.zeros(KV, dtype=np.float32)
        if h == 0:
            mask[:256] = MASKVAL
        else:
            mask[2304:] = MASKVAL
        selv = np.array([0.0, 1.0] if h == 0 else [1.0, 0.0], dtype=np.float32)
        sel = np.ascontiguousarray(np.broadcast_to(selv, (128, 2)))
        m = dict(shared)
        m.update({"x_fm": xs, "mask": mask, "sel": sel})
        in_maps.append(m)
    return in_maps


def kernel(**inputs):
    nc = build(STAGE)
    in_maps = make_in_maps(inputs)
    res = run_bass_kernel_spmd(nc, in_maps, core_ids=list(range(8)))
    out = np.empty((4, 4096, D), dtype=np.float32)
    for i in range(8):
        b, h = i // 2, i % 2
        out[b, 2048 * h:2048 * (h + 1), :] = res.results[i]["y_fm"].T
    return out


# version 15
# speedup vs baseline: 1.0470x; 1.0470x over previous
import numpy as np
from contextlib import ExitStack

import concourse.bass as bass
import concourse.mybir as mybir
from concourse import dve_ops
from concourse.bass_utils import run_bass_kernel_spmd

F32 = mybir.dt.float32
BF16 = mybir.dt.bfloat16
AF = mybir.ActivationFunctionType
ALU = mybir.AluOpType
AX = mybir.AxisListType

D = 1024
DC = 8
H1 = 1408
HC = 11
NE = 8
NX = 9
G = 4
NG = 4
T = 2048
KV = 2560
NT = 4
NKT = 20
BIG = 30000.0
MASKVAL = -1.25e8

SL_MU, SL_RCP, SL_VAR, SL_RSTD = 0, 512, 1024, 1536
SL_C1, SL_C2, SL_RDL, SL_RDG = 0, 512, 1024, 1536
R_RW = 32
R_LG = 64

STAGE = 3


class Sem:
    __slots__ = ("h", "v", "name")

    def __init__(self, h, name):
        self.h, self.v, self.name = h, 0, name


class KB:
    ENGINES = ("tensor", "vector", "scalar", "gpsimd", "sync")

    def __init__(self, nc, es):
        self.nc, self.es = nc, es
        self.q = {e: [] for e in self.ENGINES}
        self.waited = {e: {} for e in self.ENGINES}

    def sem(self, name):
        return Sem(self.es.enter_context(self.nc.semaphore(name)), name)

    def op(self, eng, meth, *a, inc=None, inck=1, low_prec=None, **kw):
        if inc is not None:
            inc.v += inck
            tok = inc.v

            def f(E, meth=meth, a=a, kw=kw, s=inc.h, k=inck, lp=low_prec):
                if lp:
                    with self.nc.allow_low_precision(reason=lp):
                        getattr(E, meth)(*a, **kw).then_inc(s, k)
                else:
                    getattr(E, meth)(*a, **kw).then_inc(s, k)
        else:
            tok = None

            def f(E, meth=meth, a=a, kw=kw, lp=low_prec):
                if lp:
                    with self.nc.allow_low_precision(reason=lp):
                        getattr(E, meth)(*a, **kw)
                else:
                    getattr(E, meth)(*a, **kw)
        self.q[eng].append(f)
        return tok

    def wait(self, eng, sem, val=None):
        if val is None:
            val = sem.v
        if val <= 0 or self.waited[eng].get(sem, -1) >= val:
            return
        self.waited[eng][sem] = val
        self.q[eng].append(lambda E, s=sem.h, v=val: E.wait_ge(s, v))


def build(stage=STAGE):
    nc = bass.Bass("TRN2", target_bir_lowering=False)

    d_x = nc.dram_tensor("x_fm", [D, T], F32, kind="ExternalInput")
    d_wqkv = nc.dram_tensor("wqkv", [D, 3 * D], F32, kind="ExternalInput")
    d_wgate = nc.dram_tensor("wgate", [D, 1], F32, kind="ExternalInput")
    d_bgate = nc.dram_tensor("bgate", [1, 1], F32, kind="ExternalInput")
    d_wout = nc.dram_tensor("wout", [D, D], F32, kind="ExternalInput")
    d_bout = nc.dram_tensor("bout", [D], F32, kind="ExternalInput")
    d_gatewT = nc.dram_tensor("gatewT", [D, NE], F32, kind="ExternalInput")
    d_gateb = nc.dram_tensor("gateb", [1, NE], F32, kind="ExternalInput")
    d_w1 = nc.dram_tensor("w1", [NE, D, H1], F32, kind="ExternalInput")
    d_b1 = nc.dram_tensor("b1", [NE, H1], F32, kind="ExternalInput")
    d_w3 = nc.dram_tensor("w3", [NE, D, H1], F32, kind="ExternalInput")
    d_b3 = nc.dram_tensor("b3", [NE, H1], F32, kind="ExternalInput")
    d_w2 = nc.dram_tensor("w2", [NE, H1, D], F32, kind="ExternalInput")
    d_sw1 = nc.dram_tensor("sw1", [D, H1], F32, kind="ExternalInput")
    d_sb1 = nc.dram_tensor("sb1", [H1], F32, kind="ExternalInput")
    d_sw3 = nc.dram_tensor("sw3", [D, H1], F32, kind="ExternalInput")
    d_sb3 = nc.dram_tensor("sb3", [H1], F32, kind="ExternalInput")
    d_sw2 = nc.dram_tensor("sw2", [H1, D], F32, kind="ExternalInput")
    d_b2aug = nc.dram_tensor("b2aug", [NX, D], F32, kind="ExternalInput")
    d_mask = nc.dram_tensor("mask", [KV], F32, kind="ExternalInput")
    d_sel = nc.dram_tensor("sel", [128, 2], F32, kind="ExternalInput")
    d_y = nc.dram_tensor("y_fm", [D, T], F32, kind="ExternalOutput")
    d_snd = nc.dram_tensor("snd", [D, 576], BF16)
    d_rcv = nc.dram_tensor("rcv", [2 * D, 576], BF16)

    es = ExitStack()
    kb = KB(nc, es)

    st = kb.sem("st")
    sa = kb.sem("sa")
    sv = kb.sem("sv")
    sg = kb.sem("sg")
    scc = kb.sem("scc")
    gi = kb.sem("gi")
    gw = kb.sem("gw")
    gw2 = kb.sem("gw2")
    gq = kb.sem("gq")
    sxq = kb.sem("sxq")
    so = kb.sem("so")
    s_mu = kb.sem("s_mu")
    s_rs = kb.sem("s_rs")
    s_rw = [kb.sem("s_rw0"), kb.sem("s_rw1")]
    s_c1 = [kb.sem("s_c1a"), kb.sem("s_c1b")]
    s_c2 = [kb.sem("s_c2a"), kb.sem("s_c2b")]

    sbt = lambda name, shape, dt: es.enter_context(nc.sbuf_tensor(name, shape, dt))

    x = [sbt(f"x{c}", [128, T], F32) for c in range(DC)]
    xn = [sbt(f"xn{c}", [128, KV], BF16) for c in range(DC)]
    pb = [es.enter_context(nc.psum_tensor(f"pb{i}", [128, 512], F32)) for i in range(8)]

    rowpool = sbt("rowpool", [128, T], F32)
    masksb = sbt("masksb", [128, NKT], F32)
    bout_sb = sbt("boutsb", [128, DC], F32)
    gatewT_sb = sbt("gatewTsb", [128, DC * NE], BF16)
    gateb_sb = sbt("gatebsb", [128, NE], F32)
    wgate_sb = sbt("wgatesb", [128, DC], BF16)
    bgate_sb = sbt("bgatesb", [1, 1], F32)
    b2aug_sb = sbt("b2augsb", [NX, D], BF16)
    sb1_sb = sbt("sb1sb", [128, HC], F32)
    sb3_sb = sbt("sb3sb", [128, HC], F32)
    sel_sb = sbt("selsb", [128, 2], F32)
    ones_bf = sbt("onesbf", [128, 1], BF16)
    iv = sbt("iv", [128, NE], F32)
    ivB = sbt("ivB", [128, NE], F32)
    lgtok = sbt("lgtok", [128, 128], F32)
    rt_a = sbt("rt_a", [128, 128], F32)
    rt_b = sbt("rt_b", [128, 128], F32)
    rt_c = sbt("rt_c", [128, 128], F32)
    rt_m = sbt("rt_m", [128, 4], F32)
    rt_m2 = sbt("rt_m2", [128, 4], F32)
    rw_bf = sbt("rwbf", [32, 512], BF16)
    xb = [sbt("xb0", [128, 512], BF16)]
    xq = [sbt("xq0", [128, 512], BF16)]
    lnbf = sbt("lnbf", [1, 1024], BF16)
    ones128r = sbt("ones128r", [1, 128], BF16)
    ubuf = [sbt("ubuf0", [128, 512], F32)]
    tbuf = [sbt("tbuf0", [128, 512], F32)]
    rwb = [sbt("rwb0", [128, 512], F32)]

    TEN, VEC, ACT, GPS, SYN = "tensor", "vector", "scalar", "gpsimd", "sync"

    all_sems = [st, sa, sv, sg, scc, gi, gw, gw2, gq, sxq, s_mu, s_rs,
                s_rw[0], s_rw[1], s_c1[0], s_c1[1], s_c2[0], s_c2[1]]

    def join_all():
        for e in KB.ENGINES:
            for s in all_sems:
                kb.wait(e, s)

    pbwar = {i: [] for i in range(8)}

    def war(bank):
        for s, v in pbwar[bank]:
            kb.wait(TEN, s, v)
        pbwar[bank] = []

    def mm(out_ap, lhsT_ap, rhs_ap, start, stop, inc=None):
        return kb.op(TEN, "matmul", out_ap, lhsT_ap, rhs_ap,
                     start=start, stop=stop, inc=inc)

    def p0(c0, w=512):
        return rowpool[0:1, c0:c0 + w]

    def grow(c0=0, w=512):
        return rowpool[96:97, c0:c0 + w]

    for c in range(DC):
        kb.op(GPS, "dma_start", out=x[c][:], in_=d_x[128 * c:128 * (c + 1), :],
              inc=gi, inck=16)
    kb.op(GPS, "dma_start", out=masksb[:],
          in_=bass.AP(d_mask, 0, [[1, 128], [128, NKT]]),
          allow_slow_non_contiguous=True, inc=gi, inck=16)
    kb.op(GPS, "dma_start", out=bout_sb[:],
          in_=bass.AP(d_bout, 0, [[1, 128], [128, DC]]),
          allow_slow_non_contiguous=True, inc=gi, inck=16)
    kb.op(GPS, "dma_start",
          out=bass.AP(gatewT_sb, 0, [[DC * NE, 128], [NE, DC], [1, NE]]),
          in_=bass.AP(d_gatewT, 0, [[NE, 128], [128 * NE, DC], [1, NE]]),
          allow_slow_non_contiguous=True, inc=gi, inck=16)
    kb.op(GPS, "dma_start", out=gateb_sb[:],
          in_=bass.AP(d_gateb, 0, [[NE, 1], [0, 128], [1, NE]]), inc=gi, inck=16)
    kb.op(GPS, "dma_start", out=bass.AP(wgate_sb, 0, [[DC, 128], [1, DC]]),
          in_=bass.AP(d_wgate, 0, [[1, 128], [128, DC]]),
          allow_slow_non_contiguous=True, inc=gi, inck=16)
    kb.op(GPS, "dma_start", out=bgate_sb[:], in_=d_bgate[:], inc=gi, inck=16)
    kb.op(GPS, "dma_start", out=b2aug_sb[:], in_=d_b2aug[:], inc=gi, inck=16)
    kb.op(GPS, "dma_start", out=sb1_sb[:],
          in_=bass.AP(d_sb1, 0, [[1, 128], [128, HC]]),
          allow_slow_non_contiguous=True, inc=gi, inck=16)
    kb.op(GPS, "dma_start", out=sb3_sb[:],
          in_=bass.AP(d_sb3, 0, [[1, 128], [128, HC]]),
          allow_slow_non_contiguous=True, inc=gi, inck=16)
    kb.op(GPS, "dma_start", out=sel_sb[:], in_=d_sel[:], inc=gi, inck=16)
    gi_all = gi.v

    kb.op(GPS, "iota", iv[:], [[1, NE]], channel_multiplier=0,
          allow_small_or_imprecise_dtypes=True, inc=sg)
    kb.wait(GPS, sg, sg.v)
    kb.op(GPS, "tensor_scalar_add", ivB[:], iv[:], BIG, inc=sg)
    tok_ivB = sg.v
    kb.op(VEC, "memset", ones_bf[:], 1.0, inc=sv)
    kb.op(VEC, "memset", ones128r[:], 1.0, inc=sv)
    tok_ones = sv.v
    kb.op(VEC, "memset", bass.AP(rowpool, R_LG * T, [[T, 32], [1, 512]]), 0.0,
          inc=sv)
    tok_lgz = sv.v
    kb.op(VEC, "memset", rw_bf[:], 1.0, inc=sv)
    tok_rwones = sv.v
    kb.op(VEC, "memset", lgtok[:], 0.0, inc=sv)
    kb.op(VEC, "memset", rt_a[:], 0.0, inc=sv)
    kb.op(VEC, "memset", rt_b[:], 0.0, inc=sv)
    kb.op(VEC, "memset", rt_c[:], 0.0, inc=sv)
    tok_rtz = sv.v

    x_tok = {(c, t): (gi, 16 * DC) for c in range(DC) for t in range(NT)}
    xn_tok = {}
    ubuf_war = [None]
    tbuf_war = [None]
    xbq_war = [None]
    rwb_war = [None]
    lnbf_war = [None]
    lg_war = [None]
    rwbf_war = [None]

    def emit_ln(t):
        xc = slice(512 * t, 512 * t + 512)
        tok_pair = {}
        for c in range(DC):
            s_, v_ = x_tok[(c, t)]
            kb.wait(ACT, s_, v_)
            kb.wait(ACT, gi, gi_all)
            if c >= 1:
                kb.wait(ACT, st, tok_pair[c - 1])
            elif xbq_war[0] is not None:
                kb.wait(ACT, xbq_war[0][0], xbq_war[0][1])
            kb.op(ACT, "copy", xb[0][:], x[c][:, xc], inc=sa)
            kb.op(ACT, "square", xq[0][:], x[c][:, xc], inc=sa)
            kb.wait(TEN, sa, sa.v)
            kb.wait(VEC, sv, tok_ones)
            if c == 0:
                war(6)
                war(7)
            mm(pb[6][0:1, :], ones_bf[:], xb[0][:], c == 0, c == DC - 1)
            tok_pair[c] = kb.op(TEN, "matmul", pb[7][0:1, :], ones_bf[:],
                                xq[0][:], start=(c == 0),
                                stop=(c == DC - 1), inc=st)
        xbq_war[0] = (st, tok_pair[DC - 1])
        tok_stats = st.v
        kb.wait(VEC, st, tok_stats)
        kb.op(VEC, "tensor_scalar_mul", p0(SL_MU), pb[6][0:1, :], 1.0 / D, inc=sv)
        kb.wait(VEC, sv, sv.v)
        kb.op(VEC, "tensor_mul", p0(SL_RSTD), p0(SL_MU), p0(SL_MU), inc=sv)
        kb.wait(VEC, sv, sv.v)
        kb.op(VEC, "scalar_tensor_tensor", p0(SL_VAR), pb[7][0:1, :], 1.0 / D,
              p0(SL_RSTD), ALU.mult, ALU.subtract, inc=sv)
        tok_var = sv.v
        kb.wait(VEC, sv, tok_var)
        kb.op(VEC, "tensor_scalar_add", p0(SL_VAR), p0(SL_VAR), 1e-5, inc=sv)
        kb.wait(VEC, sv, sv.v)
        kb.op(VEC, "reciprocal", p0(SL_RSTD), p0(SL_VAR), inc=sv)
        tok_rcp = sv.v
        kb.wait(ACT, sv, tok_rcp)
        kb.op(ACT, "activation", p0(SL_RCP), p0(SL_RSTD), AF.Sqrt,
              bias=0.0, scale=1.0, inc=sa)
        kb.wait(ACT, sa, sa.v)
        if lnbf_war[0] is not None:
            kb.wait(ACT, lnbf_war[0][0], lnbf_war[0][1])
        kb.op(ACT, "copy", lnbf[:], p0(SL_MU, 1024), inc=sa)
        tok_lnbf = sa.v
        kb.wait(TEN, sa, tok_lnbf)
        mm(pb[6][:], ones128r[:], lnbf[0:1, 0:512], True, True)
        tkbc = mm(pb[7][:], ones128r[:], lnbf[0:1, 512:1024], True, True,
                  inc=st)
        lnbf_war[0] = (st, tkbc)
        for c in range(DC):
            s_, v_ = x_tok[(c, t)]
            kb.wait(VEC, s_, v_)
            kb.wait(VEC, st, tkbc)
            if tbuf_war[0] is not None:
                kb.wait(VEC, tbuf_war[0][0], tbuf_war[0][1])
            kb.wait(VEC, sv, sv.v)
            kb.op(VEC, "tensor_tensor", tbuf[0][:], x[c][:, xc], pb[6][:],
                  ALU.subtract, inc=sv)
            kb.wait(VEC, sv, sv.v)
            kb.wait(VEC, st, st.v)
            kb.wait(VEC, sxq, sxq.v)
            kb.op(VEC, "tensor_tensor", xn[c][:, 256 + 512 * t:256 + 512 * t + 512],
                  tbuf[0][:], pb[7][:], ALU.mult, inc=sv)
            xn_tok[(c, t)] = sv.v
            tbuf_war[0] = (sv, sv.v)
        pbwar[6].append((sv, sv.v))
        pbwar[7].append((sv, sv.v))
        return sv.v

    def emit_router(t):
        tc = 256 + 512 * t
        for c in range(DC):
            kb.wait(TEN, sv, xn_tok[(c, t)])
            if c == 0:
                war(6)
            mm(pb[6][0:NE, :], gatewT_sb[:, NE * c:NE * (c + 1)],
               xn[c][:, tc:tc + 512], c == 0, c == DC - 1,
               inc=st if c == DC - 1 else None)
        kb.wait(ACT, st, st.v)
        kb.wait(ACT, sv, tok_lgz)
        if lg_war[0] is not None:
            kb.wait(ACT, lg_war[0][0], lg_war[0][1])
        kb.op(ACT, "copy", bass.AP(rowpool, R_LG * T, [[T, NE], [1, 512]]),
              pb[6][0:NE, :], inc=sa)
        tok_lg = sa.v
        pbwar[6].append((sa, tok_lg))
        kb.wait(VEC, sa, tok_lg)
        kb.wait(VEC, sv, tok_rtz)
        for s_ in range(4):
            for J in range(4):
                kb.op(VEC, "transpose",
                      lgtok[J * 32:(J + 1) * 32, 32 * s_:32 * (s_ + 1)],
                      bass.AP(rowpool, R_LG * T + 128 * s_ + 32 * J,
                              [[T, 32], [1, 32]]),
                      inc=sv if (s_ == 3 and J == 3) else None)
        lg_war[0] = (sv, sv.v)
        kb.wait(VEC, sv, sv.v)
        v3 = lambda tns: bass.AP(tns, 0, [[128, 128], [32, 4], [1, NE]])
        sub = lambda tns, s_: tns[:, 32 * s_:32 * s_ + NE]

        def for4(meth, mk):
            for s_ in range(4):
                kb.op(VEC, meth, *mk(s_), inc=sv if s_ == 3 else None)
            kb.wait(VEC, sv, sv.v)

        kb.op(VEC, "tensor_reduce", rt_m[:], v3(lgtok), AX.X, ALU.max,
              negate=True, inc=sv)
        kb.wait(VEC, sv, sv.v)
        kb.wait(ACT, sv, sv.v)
        for s_ in range(4):
            kb.op(ACT, "activation", sub(rt_a, s_), sub(lgtok, s_), AF.Exp,
                  bias=rt_m[:, s_:s_ + 1], scale=1.0,
                  inc=sa if s_ == 3 else None)
        kb.wait(VEC, sa, sa.v)
        kb.op(VEC, "tensor_reduce", rt_m2[:], v3(rt_a), AX.X, ALU.add, inc=sv)
        kb.wait(VEC, sv, sv.v)
        kb.op(VEC, "reciprocal", rt_m[:], rt_m2[:], inc=sv)
        kb.wait(VEC, sv, sv.v)
        for4("tensor_scalar_mul",
             lambda s_: (sub(rt_a, s_), sub(rt_a, s_), rt_m[:, s_:s_ + 1]))
        for4("tensor_add",
             lambda s_: (sub(rt_b, s_), sub(rt_a, s_), gateb_sb[:]))
        kb.op(VEC, "tensor_reduce", rt_m[:], v3(rt_b), AX.X, ALU.max, inc=sv)
        kb.wait(VEC, sv, sv.v)
        kb.wait(VEC, sg, tok_ivB)
        for4("tensor_scalar",
             lambda s_: (sub(rt_c, s_), sub(rt_b, s_), rt_m[:, s_:s_ + 1], None,
                         ALU.is_ge))
        for4("scalar_tensor_tensor",
             lambda s_: (sub(rt_c, s_), sub(rt_c, s_), -BIG, ivB[:],
                         ALU.mult, ALU.add))
        kb.op(VEC, "tensor_reduce", rt_m[:], v3(rt_c), AX.X, ALU.min, inc=sv)
        kb.wait(VEC, sv, sv.v)
        for4("tensor_scalar",
             lambda s_: (sub(rt_c, s_), iv[:], rt_m[:, s_:s_ + 1], None,
                         ALU.is_equal))
        for4("scalar_tensor_tensor",
             lambda s_: (sub(rt_b, s_), sub(rt_c, s_), -BIG, sub(rt_b, s_),
                         ALU.mult, ALU.add))
        kb.op(VEC, "tensor_reduce", rt_m2[:], v3(rt_b), AX.X, ALU.max, inc=sv)
        kb.wait(VEC, sv, sv.v)
        for4("tensor_scalar",
             lambda s_: (sub(rt_b, s_), sub(rt_b, s_), rt_m2[:, s_:s_ + 1], None,
                         ALU.is_ge))
        for4("scalar_tensor_tensor",
             lambda s_: (sub(rt_b, s_), sub(rt_b, s_), -BIG, ivB[:],
                         ALU.mult, ALU.add))
        kb.op(VEC, "tensor_reduce", rt_m2[:], v3(rt_b), AX.X, ALU.min, inc=sv)
        kb.wait(VEC, sv, sv.v)
        for4("tensor_scalar",
             lambda s_: (sub(rt_b, s_), iv[:], rt_m2[:, s_:s_ + 1], None,
                         ALU.is_equal))
        for4("tensor_add",
             lambda s_: (sub(rt_b, s_), sub(rt_b, s_), sub(rt_c, s_)))
        for4("tensor_mul",
             lambda s_: (sub(rt_c, s_), sub(rt_a, s_), sub(rt_b, s_)))
        kb.wait(VEC, s_rw[0], s_rw[0].v)
        kb.wait(VEC, s_rw[1], s_rw[1].v)
        for s_ in range(4):
            for a_ in range(4):
                kb.op(VEC, "transpose",
                      bass.AP(rowpool, R_RW * T + 512 * t + 128 * s_ + 32 * a_,
                              [[T, 32], [1, 32]]),
                      rt_c[32 * a_:32 * (a_ + 1), 32 * s_:32 * (s_ + 1)],
                      inc=sv if (s_ == 3 and a_ == 3) else None)
        tok_rwT = sv.v
        kb.wait(ACT, sv, tok_rwT)
        if rwbf_war[0] is not None:
            kb.wait(ACT, rwbf_war[0][0], rwbf_war[0][1])
        kb.op(ACT, "copy", rw_bf[0:NE, 0:512],
              bass.AP(rowpool, R_RW * T + 512 * t, [[T, NE], [1, 512]]), inc=sa)
        return tok_rwT, sa.v

    pass_state = {"p": 0, "p1_last": {}, "p2_last": {},
                  "p1a_last": {}, "p1v_last": {}}

    def emit_ffn_pass():
        p = pass_state["p"]
        pass_state["p"] += 1
        p1_last = pass_state["p1_last"]
        p2_last = pass_state["p2_last"]
        p1a_last = pass_state["p1a_last"]
        p1v_last = pass_state["p1v_last"]
        join_all()
        fes = ExitStack()
        fsb = lambda name, shape, dt: fes.enter_context(
            nc.sbuf_tensor(f"{name}_p{p}", shape, dt))
        w1b = [fsb(f"w1b{k}", [128, H1], BF16) for k in range(DC)]
        w3b = [fsb(f"w3b{k}", [128, H1], BF16) for k in range(DC)]
        w2b = [fsb(f"w2b{k}", [128, D], BF16) for k in range(HC)]
        b1sb = fsb("b1sb", [128, HC], F32)
        b3sb = fsb("b3sb", [128, HC], F32)
        hb = [fsb(f"hb{k}", [128, 512], BF16) for k in range(HC)]
        rw_toks = {}

        def emit_A(t):
            emit_ln(t)
            rw_toks[t] = emit_router(t)

        def emit_bias(t):
            xc = slice(512 * t, 512 * t + 512)
            kb.wait(TEN, sa, rw_toks[t][1])
            kb.wait(TEN, sv, tok_rwones)
            last_bias_tok = None
            for mD in range(DC):
                war(mD)
                tk = mm(pb[mD][:], b2aug_sb[0:NX, 128 * mD:128 * (mD + 1)],
                        rw_bf[0:NX, 0:512], True, True, inc=st)
                last_bias_tok = tk
                kb.wait(VEC, st, tk)
                s_, v_ = x_tok[(mD, t)]
                kb.wait(VEC, s_, v_)
                kb.op(VEC, "scalar_tensor_tensor", x[mD][:, xc], pb[mD][:],
                      0.5, x[mD][:, xc], ALU.mult, ALU.add, inc=sv)
                x_tok[(mD, t)] = (sv, sv.v)
                pbwar[mD].append((sv, sv.v))
            rwbf_war[0] = (st, last_bias_tok)
        h_tok = {k: None for k in range(HC)}
        tok_p2_all = [None]
        for epos, e in enumerate(range(NX)):
            key = (p, epos)
            prev = (p, epos - 1) if epos > 0 else (p - 1, NX - 1)
            if prev in p1_last:
                kb.wait(GPS, st, p1_last[prev])
            if prev in p1a_last:
                kb.wait(GPS, sa, p1a_last[prev])
            if prev in p1v_last:
                kb.wait(GPS, sv, p1v_last[prev])
            for k in range(DC):
                src1 = d_w1[e, 128 * k:128 * (k + 1), :] if e < NE else \
                    d_sw1[128 * k:128 * (k + 1), :]
                src3 = d_w3[e, 128 * k:128 * (k + 1), :] if e < NE else \
                    d_sw3[128 * k:128 * (k + 1), :]
                kb.op(GPS, "dma_start", out=w1b[k][:], in_=src1, inc=gw, inck=16)
                kb.op(GPS, "dma_start", out=w3b[k][:], in_=src3, inc=gw, inck=16)
            if e < NE:
                kb.op(GPS, "dma_start", out=b1sb[:],
                      in_=bass.AP(d_b1, e * H1, [[1, 128], [128, HC]]),
                      allow_slow_non_contiguous=True, inc=gw, inck=16)
                kb.op(GPS, "dma_start", out=b3sb[:],
                      in_=bass.AP(d_b3, e * H1, [[1, 128], [128, HC]]),
                      allow_slow_non_contiguous=True, inc=gw, inck=16)
            gw_tok = gw.v
            if prev in p2_last:
                kb.wait(GPS, st, p2_last[prev])
            for k in range(HC):
                src2 = d_w2[e, 128 * k:128 * (k + 1), :] if e < NE else \
                    d_sw2[128 * k:128 * (k + 1), :]
                kb.op(GPS, "dma_start", out=w2b[k][:], in_=src2, inc=gw2, inck=16)
            gw2_tok = gw2.v
            bcol1 = (lambda m: b1sb[:, m:m + 1]) if e < NE else \
                (lambda m: sb1_sb[:, m:m + 1])
            bcol3 = (lambda m: b3sb[:, m:m + 1]) if e < NE else \
                (lambda m: sb3_sb[:, m:m + 1])
            for t in range(NT):
                if epos == 0 and t not in rw_toks:
                    emit_A(t)
                tc = 256 + 512 * t
                xc = slice(512 * t, 512 * t + 512)
                if e < NE:
                    if rwb_war[0] is not None:
                        kb.wait(SYN, rwb_war[0][0], rwb_war[0][1])
                    kb.wait(SYN, sv, rw_toks[t][0])
                    kb.op(SYN, "dma_start", out=rwb[0][:],
                          in_=bass.AP(rowpool, (R_RW + e) * T + 512 * t,
                                      [[T, 1], [0, 128], [1, 512]]),
                          inc=s_rw[0], inck=16)
                    rw_dma_tok = s_rw[0].v
                for m in range(HC):
                    bA, bB = m % 3, 3 + m % 3
                    war(bA)
                    war(bB)
                    kb.wait(TEN, gw, gw_tok)
                    for c in range(DC):
                        kb.wait(TEN, sv, xn_tok[(c, t)])
                    tokA = None
                    for k in range(DC):
                        tokA = mm(pb[bA][:], w1b[k][:, 128 * m:128 * (m + 1)],
                                  xn[k][:, tc:tc + 512], k == 0, k == DC - 1,
                                  inc=st if k == DC - 1 else None)
                    tokB = None
                    for k in range(DC):
                        tokB = mm(pb[bB][:], w3b[k][:, 128 * m:128 * (m + 1)],
                                  xn[k][:, tc:tc + 512], k == 0, k == DC - 1,
                                  inc=st if k == DC - 1 else None)
                    p1_last[key] = tokB
                    kb.wait(ACT, st, tokA)
                    kb.wait(ACT, gw, gw_tok)
                    if ubuf_war[0] is not None:
                        kb.wait(ACT, ubuf_war[0][0], ubuf_war[0][1])
                    kb.op(ACT, "activation", ubuf[0][:], pb[bA][:], AF.Identity,
                          bias=bcol1(m), scale=1.0, inc=sa)
                    tokU = sa.v
                    p1a_last[key] = tokU
                    pbwar[bA].append((sa, tokU))
                    kb.wait(VEC, st, tokB)
                    kb.wait(VEC, sa, tokU)
                    if tbuf_war[0] is not None:
                        kb.wait(VEC, tbuf_war[0][0], tbuf_war[0][1])
                    kb.op(VEC, "scalar_tensor_tensor", tbuf[0][:], pb[bB][:],
                          bcol3(m), ubuf[0][:], ALU.add, ALU.mult, inc=sv)
                    tokT = sv.v
                    p1v_last[key] = tokT
                    pbwar[bB].append((sv, tokT))
                    kb.wait(ACT, sv, tokT)
                    if tok_p2_all[0] is not None:
                        kb.wait(ACT, st, tok_p2_all[0])
                    if e == NE:
                        kb.op(ACT, "activation", hb[m][:], tbuf[0][:],
                              AF.Silu, inc=sa)
                        h_tok[m] = (sa, sa.v)
                        tbuf_war[0] = (sa, sa.v)
                        ubuf_war[0] = (sv, tokT)
                    else:
                        kb.op(ACT, "activation", ubuf[0][:], tbuf[0][:],
                              AF.Silu, inc=sa)
                        tokS = sa.v
                        tbuf_war[0] = (sa, tokS)
                        kb.wait(VEC, sa, tokS)
                        kb.wait(VEC, s_rw[0], rw_dma_tok)
                        if tok_p2_all[0] is not None:
                            kb.wait(VEC, st, tok_p2_all[0])
                        kb.op(VEC, "tensor_tensor", hb[m][:], ubuf[0][:],
                              rwb[0][:], ALU.mult, inc=sv)
                        h_tok[m] = (sv, sv.v)
                        ubuf_war[0] = (sv, sv.v)
                if e < NE:
                    rwb_war[0] = h_tok[HC - 1]
                if epos == 0:
                    emit_bias(t)
                tok_y = {}
                for k in range(HC):
                    kb.wait(TEN, gw2, gw2_tok)
                    kb.wait(TEN, h_tok[k][0], h_tok[k][1])
                    for mD in range(DC):
                        if k == 0:
                            war(mD)
                        tk = mm(pb[mD][:], w2b[k][:, 128 * mD:128 * (mD + 1)],
                                hb[k][:], k == 0, k == HC - 1,
                                inc=st if k == HC - 1 else None)
                        if k == HC - 1:
                            tok_y[mD] = tk
                p2_last[key] = tok_y[DC - 1]
                tok_p2_all[0] = tok_y[DC - 1]
                for mD in range(DC):
                    kb.wait(VEC, st, tok_y[mD])
                    s_, v_ = x_tok[(mD, t)]
                    kb.wait(VEC, s_, v_)
                    kb.op(VEC, "scalar_tensor_tensor", x[mD][:, xc], pb[mD][:],
                          0.5, x[mD][:, xc], ALU.mult, ALU.add, inc=sv)
                    x_tok[(mD, t)] = (sv, sv.v)
                    pbwar[mD].append((sv, sv.v))
        join_all()
        fes.close()

    def emit_attention():
        join_all()
        aes = ExitStack()
        asb = lambda name, shape, dt: aes.enter_context(
            nc.sbuf_tensor(name, shape, dt))
        wq_s = [asb(f"wqs{c}", [128, 256], BF16) for c in range(DC)]
        wk_s = [asb(f"wks{c}", [128, 256], BF16) for c in range(DC)]
        wv_s = [asb(f"wvs{c}", [128, 256], BF16) for c in range(DC)]
        wout_g = [asb(f"woutg{j}", [128, D], BF16) for j in range(2)]
        q_sb = [asb(f"qsb{j}", [128, T], BF16) for j in range(2)]
        k_sb = [asb(f"ksb{j}", [128, KV], BF16) for j in range(2)]
        v_sb = asb("vsb", [128, NKT * 260], BF16)
        vstr = asb("vstr", [128, 272], BF16)
        kstr = asb("kstr", [128, 256], BF16)
        xnp = [asb(f"xnp{c}", [128, 64], BF16) for c in range(DC)]
        expS = asb("expS", [128, 8 * 512], BF16)
        expSg = [asb("expSg0", [128, 512], BF16)]
        rcpbf = asb("rcpbf", [1, 1024], BF16)
        ones64 = asb("ones64", [1, 64], BF16)
        gbb1m = asb("gbb1m", [64, 512], F32)
        cbuf1 = asb("cbuf1", [64, 512], F32)
        cbuf2 = asb("cbuf2", [64, 512], F32)
        gbb = asb("gbb", [64, 512], F32)
        comb = [asb(f"comb{j}", [128, T], BF16) for j in range(2)]
        exA = asb("exA", [128, 576], BF16)
        exB = asb("exB", [128, 576], BF16)
        xstr = exB
        for t in range(NT):
            emit_ln(t)
        for t in range(NT):
            tc = 256 + 512 * t
            war(4)
            for c in range(DC):
                kb.wait(TEN, sv, xn_tok[(c, t)])
                mm(pb[4][0:1, :], wgate_sb[:, c:c + 1], xn[c][:, tc:tc + 512],
                   c == 0, c == DC - 1, inc=st if c == DC - 1 else None)
            kb.wait(ACT, st, st.v)
            kb.op(ACT, "activation", grow(512 * t), pb[4][0:1, :],
                  AF.Sigmoid, bias=bgate_sb[0:1, 0:1], scale=1.0, inc=sa)
            pbwar[4].append((sa, sa.v))
        tok_g = sa.v

        qkv_last = {}
        oproj_last = {}
        gq_toks = {}

        def emit_group_loads(g):
            if g - 1 in qkv_last:
                kb.wait(GPS, st, qkv_last[g - 1])
            for c in range(DC):
                kb.op(GPS, "dma_start", out=wq_s[c][:],
                      in_=d_wqkv[128 * c:128 * (c + 1), 256 * g:256 * (g + 1)],
                      inc=gq, inck=16)
                kb.op(GPS, "dma_start", out=wk_s[c][:],
                      in_=d_wqkv[128 * c:128 * (c + 1),
                                 D + 256 * g:D + 256 * (g + 1)],
                      inc=gq, inck=16)
                kb.op(GPS, "dma_start", out=wv_s[c][:],
                      in_=d_wqkv[128 * c:128 * (c + 1),
                                 2 * D + 256 * g:2 * D + 256 * (g + 1)],
                      inc=gq, inck=16)
            if g - 1 in oproj_last:
                kb.wait(GPS, st, oproj_last[g - 1])
            for j in range(2):
                kb.op(GPS, "dma_start", out=wout_g[j][:],
                      in_=d_wout[256 * g + 128 * j:256 * g + 128 * (j + 1), :],
                      inc=gq, inck=16)
            gq_toks[g] = gq.v

        kb.op(VEC, "memset", ones64[:], 1.0, inc=sv)
        tok_ones64 = sv.v
        emit_group_loads(0)

        for c in range(DC):
            for t in range(NT):
                kb.wait(SYN, sv, xn_tok[(c, t)])
        for c in range(DC):
            for t in range(NT):
                kb.wait(VEC, sv, xn_tok[(c, t)])
            kb.op(VEC, "tensor_copy", xstr[:, 64 * c:64 * (c + 1)],
                  bass.AP(xn[c], 256, [[KV, 128], [32, 64]]), inc=sv)
        tok_xstr = sv.v
        kb.wait(SYN, sv, tok_xstr)
        for c in range(DC):
            kb.op(SYN, "dma_start", out=d_snd[128 * c:128 * (c + 1), 0:256],
                  in_=xn[c][:, 256:512], inc=sxq, inck=16)
            kb.op(SYN, "dma_start", out=d_snd[128 * c:128 * (c + 1), 256:512],
                  in_=xn[c][:, 2048:2304], inc=sxq, inck=16)
            kb.op(SYN, "dma_start", out=d_snd[128 * c:128 * (c + 1), 512:576],
                  in_=xstr[:, 64 * c:64 * (c + 1)], inc=sxq, inck=16)
        kb.wait(GPS, sxq, sxq.v)
        kb.op(GPS, "collective_compute", "AllGather", ALU.bypass,
              replica_groups=[[0, 1], [2, 3], [4, 5], [6, 7]],
              ins=[d_snd[:]], outs=[d_rcv[:]], inc=scc)
        kb.wait(SYN, scc, scc.v)
        for c in range(DC):
            kb.op(SYN, "dma_start", out=exA[:],
                  in_=d_rcv[128 * c:128 * (c + 1), :], inc=sxq, inck=16)
            kb.op(SYN, "dma_start", out=exB[:],
                  in_=d_rcv[D + 128 * c:D + 128 * (c + 1), :], inc=sxq, inck=16)
            kb.wait(VEC, sxq, sxq.v)
            kb.wait(VEC, gi, gi_all)
            kb.op(VEC, "tensor_scalar_mul", exA[:], exA[:], sel_sb[:, 0:1],
                  inc=sv)
            kb.wait(VEC, sv, sv.v)
            kb.op(VEC, "scalar_tensor_tensor", exA[:], exB[:], sel_sb[:, 1:2],
                  exA[:], ALU.mult, ALU.add, inc=sv)
            tok_m = sv.v
            kb.wait(SYN, sv, tok_m)
            kb.op(SYN, "dma_start", out=xn[c][:, 0:256], in_=exA[:, 256:512],
                  inc=sxq, inck=16)
            kb.op(SYN, "dma_start", out=xn[c][:, 2304:2560], in_=exA[:, 0:256],
                  inc=sxq, inck=16)
            kb.op(SYN, "dma_start", out=xnp[c][:], in_=exA[:, 512:576],
                  inc=sxq, inck=16)
            kb.wait(SYN, sxq, sxq.v)
        tok_halo = sxq.v

        comb_war = {0: [], 1: []}
        qk_war = []
        expS_war = {}
        expSg_war = [None]
        prev_opl = [None]
        gbb_war = [None]
        gbb1m_tok = [0]
        rcpbf_war = [None]
        rcpslot_war = [None]
        for g in range(NG):
            if g > 0:
                emit_group_loads(g)
            gq_tok = gq_toks[g]

            rot_i = [0]

            def acc_evac(lhs_fn, rhs_fn, out_fn, n_acc, width=512, m_rows=128):
                bank = rot_i[0] % 4
                rot_i[0] += 1
                war(bank)
                tk = None
                for c in range(n_acc):
                    tk = mm(pb[bank][0:m_rows, 0:width], lhs_fn(c), rhs_fn(c),
                            c == 0, c == n_acc - 1,
                            inc=st if c == n_acc - 1 else None)
                kb.wait(ACT, st, tk)
                kb.op(ACT, "copy", out_fn(), pb[bank][0:m_rows, 0:width], inc=sa)
                pbwar[bank].append((sa, sa.v))
                return tk, sa.v

            kb.wait(TEN, gq, gq_tok)
            for s_, v_ in qk_war:
                kb.wait(ACT, s_, v_)
            qk_war = []
            last_qkv = None
            for mq in range(2):
                for t in range(NT):
                    for c in range(DC):
                        kb.wait(TEN, sv, xn_tok[(c, t)])
                    tmm, _ = acc_evac(
                        lambda c, mq=mq: wq_s[c][:, 128 * mq:128 * (mq + 1)],
                        lambda c, t=t: xn[c][:, 256 + 512 * t:256 + 512 * t + 512],
                        lambda mq=mq, t=t: q_sb[mq][:, 512 * t:512 * t + 512], DC)
                    last_qkv = tmm

            def k_part(mk, kt):
                return acc_evac(
                    lambda c, mk=mk: wk_s[c][:, 128 * mk:128 * (mk + 1)],
                    lambda c, kt=kt: xn[c][:, 512 * kt:512 * kt + 512],
                    lambda mk=mk, kt=kt: k_sb[mk][:, 512 * kt:512 * kt + 512],
                    DC)

            def v_part(vt):
                tmm, tev = acc_evac(
                    lambda c, vt=vt: xn[c][:, 128 * vt:128 * (vt + 1)],
                    lambda c: wv_s[c][:],
                    lambda vt=vt: bass.AP(v_sb, 260 * vt,
                                          [[NKT * 260, 128], [65, 4], [1, 64]]),
                    DC, width=256)
                kb.wait(VEC, sa, tev)
                if prev_opl[0] is not None:
                    kb.wait(VEC, st, prev_opl[0])
                kb.op(VEC, "memset",
                      bass.AP(v_sb, 260 * vt + 64,
                              [[NKT * 260, 128], [65, 4], [1, 1]]), 1.0, inc=sv)
                return tmm, (sv, sv.v)

            v_ev = []
            for mk in range(2):
                for kt in (1, 2, 3):
                    tmm, tev = k_part(mk, kt)
                    last_qkv = tmm
            for vt in range(2, 18):
                tmm, ev = v_part(vt)
                last_qkv = tmm
                v_ev.append(ev)
            tmm, vstr_o_tok = acc_evac(
                lambda c: bass.AP(xn[c], 256, [[KV, 128], [32, 64]]),
                lambda c: wv_s[c][:],
                lambda: bass.AP(vstr, 0, [[272, 64], [65, 4], [1, 64]]),
                DC, width=256, m_rows=64)
            last_qkv = tmm
            kb.wait(TEN, sxq, tok_halo)
            k_ev_tok = None
            for mk in range(2):
                for kt in (0, 4):
                    tmm, tev = k_part(mk, kt)
                    last_qkv = tmm
                    k_ev_tok = tev
            for vt in (0, 1, 18, 19):
                tmm, ev = v_part(vt)
                last_qkv = tmm
                v_ev.append(ev)
            kstr_p_tok = None
            for mk in range(2):
                tmm, tev = acc_evac(
                    lambda c, mk=mk: wk_s[c][:, 128 * mk:128 * (mk + 1)],
                    lambda c: xnp[c][:],
                    lambda mk=mk: kstr[:, 128 * mk + 64:128 * mk + 128],
                    DC, width=64)
                last_qkv = tmm
                kstr_p_tok = tev
            tmm, vstr_p_tok = acc_evac(
                lambda c: xnp[c][:],
                lambda c: wv_s[c][:],
                lambda: bass.AP(vstr, 64 * 272, [[272, 64], [65, 4], [1, 64]]),
                DC, width=256, m_rows=64)
            last_qkv = tmm
            kb.wait(VEC, sa, vstr_p_tok)
            kb.wait(VEC, sa, vstr_o_tok)
            if prev_opl[0] is not None:
                kb.wait(VEC, st, prev_opl[0])
            kb.op(VEC, "memset",
                  bass.AP(vstr, 64, [[272, 128], [65, 4], [1, 1]]),
                  1.0, inc=sv)
            vstr_ones_tok = sv.v
            qkv_last[g] = last_qkv
            if prev_opl[0] is not None:
                kb.wait(VEC, st, prev_opl[0])
            kb.wait(VEC, sa, k_ev_tok)
            for ch in range(2):
                kb.op(VEC, "tensor_copy", kstr[:, 128 * ch:128 * ch + 64],
                      bass.AP(k_sb[ch], 256, [[KV, 128], [32, 64]]), inc=sv)
            str_tok = sv.v

            combine_last = None
            pend = None
            pend2 = None

            def emit_pv(P):
                war(4 + P["par"])
                kb.wait(TEN, sa, P["tokE"][DC - 1])
                tkPV = None
                for kc in range(DC):
                    tkPV = mm(pb[4 + P["par"]][0:65, :],
                              v_sb[:, 260 * (4 * P["w"] + kc) + 65 * P["j"]:
                                   260 * (4 * P["w"] + kc) + 65 * (P["j"] + 1)],
                              expS[:, 512 * kc:512 * (kc + 1)],
                              kc == 0, kc == DC - 1, inc=st)
                    expS_war[kc] = (st, tkPV)
                kb.wait(TEN, sa, P["tokEg"])
                kb.wait(TEN, sv, vstr_ones_tok)
                kb.wait(TEN, sa, vstr_p_tok)
                kb.wait(TEN, sa, vstr_o_tok)
                gbank = 6 + P["par"]
                war(gbank)
                tkPVg = mm(pb[gbank][0:65, :],
                           vstr[:, 65 * P["j"]:65 * (P["j"] + 1)],
                           expSg[0][:], True, True, inc=st)
                expSg_war[0] = (st, tkPVg)
                P["tkPV"], P["tkPVg"] = tkPV, tkPVg

            def emit_combine_a(P):
                par, j, w = P["par"], P["j"], P["w"]
                if j == 0:
                    if gbb_war[0] is not None:
                        kb.wait(SYN, gbb_war[0][0], gbb_war[0][1])
                    kb.wait(SYN, sa, tok_g)
                    kb.op(SYN, "dma_start", out=gbb[:],
                          in_=bass.AP(rowpool, 96 * T + 512 * w,
                                      [[T, 1], [0, 64], [1, 512]]),
                          inc=s_c1[1], inck=16)
                    kb.wait(VEC, s_c1[1], s_c1[1].v)
                    kb.op(VEC, "tensor_scalar", gbb1m[:], gbb[:], -1.0, 1.0,
                          ALU.mult, ALU.add, inc=sv)
                    gbb1m_tok[0] = sv.v
                kb.wait(VEC, st, P["tkPV"])
                if rcpslot_war[0] is not None:
                    kb.wait(VEC, rcpslot_war[0][0], rcpslot_war[0][1])
                kb.op(VEC, "reciprocal", p0(SL_C1), pb[4 + par][64:65, :],
                      inc=sv)
                pbwar[4 + par].append((sv, sv.v))
                kb.wait(VEC, st, P["tkPVg"])
                kb.op(VEC, "reciprocal", p0(SL_C2), pb[6 + par][64:65, :],
                      inc=sv)
                pbwar[6 + par].append((sv, sv.v))
                tok_rcp = sv.v
                kb.wait(ACT, sv, tok_rcp)
                if rcpbf_war[0] is not None:
                    kb.wait(ACT, rcpbf_war[0][0], rcpbf_war[0][1])
                kb.op(ACT, "copy", rcpbf[:], p0(SL_C1, 1024), inc=sa)
                P["tok_rbf"] = sa.v
                rcpslot_war[0] = (sa, sa.v)

            def emit_combine(P):
                par, j, w, ch, ro = P["par"], P["j"], P["w"], P["ch"], P["ro"]
                bb = 0
                kb.wait(TEN, sa, P["tok_rbf"])
                kb.wait(TEN, sv, tok_ones64)
                war(bb)
                mm(pb[bb][0:64, :], ones64[:], rcpbf[0:1, 0:512], True, True)
                tkB = mm(pb[bb][64:128, :], ones64[:], rcpbf[0:1, 512:1024],
                         True, True, inc=st)
                rcpbf_war[0] = (st, tkB)
                kb.wait(VEC, st, tkB)
                kb.wait(VEC, sv, gbb1m_tok[0])
                kb.wait(VEC, sv, sv.v)
                kb.op(VEC, "tensor_tensor", cbuf1[:], gbb[:], pb[bb][0:64, :],
                      ALU.mult, inc=sv)
                kb.wait(VEC, sv, sv.v)
                kb.op(VEC, "tensor_tensor", cbuf2[:], gbb1m[:],
                      pb[bb][64:128, :], ALU.mult, inc=sv)
                gbb_war[0] = (sv, sv.v)
                pbwar[bb].append((sv, sv.v))
                kb.wait(VEC, st, P["tkPV"])
                kb.wait(VEC, st, P["tkPVg"])
                kb.wait(VEC, sv, sv.v)
                kb.op(VEC, "tensor_tensor", cbuf1[:], pb[4 + par][0:64, :],
                      cbuf1[:], ALU.mult, inc=sv)
                pbwar[4 + par].append((sv, sv.v))
                kb.wait(VEC, sv, sv.v)
                kb.op(VEC, "tensor_tensor", cbuf2[:], pb[6 + par][0:64, :],
                      cbuf2[:], ALU.mult, inc=sv)
                pbwar[6 + par].append((sv, sv.v))
                kb.wait(VEC, sv, sv.v)
                for s_, v_ in comb_war[ch]:
                    kb.wait(VEC, s_, v_)
                comb_war[ch] = []
                kb.op(VEC, "tensor_add",
                      comb[ch][ro:ro + 64, 512 * w:512 * (w + 1)],
                      cbuf1[:], cbuf2[:], inc=sv)
                return sv.v

            def emit_s_half(i, kcs, tokS):
                w, j = i // 4, i % 4
                ch, ro = j // 2, 64 * (j % 2)
                for kc in kcs:
                    bank = kc % 4
                    war(bank)
                    kb.wait(TEN, sa, k_ev_tok)
                    tokS[kc] = mm(
                        pb[bank][:],
                        k_sb[ch][ro:ro + 64,
                                 512 * w + 128 * kc:512 * w + 128 * (kc + 1)],
                        q_sb[ch][ro:ro + 64, 512 * w:512 * (w + 1)],
                        True, True, inc=st)

            def emit_exp_half(i, kcs, tokS, tokE):
                w = i // 4
                if (DC - 1) in expS_war:
                    kb.wait(ACT, expS_war[DC - 1][0], expS_war[DC - 1][1])
                for kc in kcs:
                    bank = kc % 4
                    kb.wait(ACT, st, tokS[kc])
                    kb.op(ACT, "activation", expS[:, 512 * kc:512 * (kc + 1)],
                          pb[bank][:], AF.Exp,
                          bias=masksb[:, 4 * w + kc:4 * w + kc + 1], scale=0.125,
                          inc=sa)
                    tokE[kc] = sa.v
                    pbwar[bank].append((sa, sa.v))

            for i in range(16):
                w, j = i // 4, i % 4
                ch, ro = j // 2, 64 * (j % 2)
                par = i % 2
                tokS, tokE = {}, {}
                if pend2 is not None:
                    emit_combine_a(pend2)
                emit_s_half(i, (0, 1, 2, 3), tokS)
                if pend is not None:
                    emit_pv(pend)
                emit_exp_half(i, (0, 1, 2, 3), tokS, tokE)
                emit_s_half(i, (4, 5, 6, 7), tokS)
                emit_exp_half(i, (4, 5, 6, 7), tokS, tokE)
                war(0)
                kb.wait(TEN, sv, str_tok)
                kb.wait(TEN, sa, kstr_p_tok)
                tkSg = mm(pb[0][:], kstr[ro:ro + 64, 128 * ch:128 * (ch + 1)],
                          q_sb[ch][ro:ro + 64, 512 * w:512 * (w + 1)],
                          True, True, inc=st)
                kb.wait(ACT, st, tkSg)
                if expSg_war[0] is not None:
                    kb.wait(ACT, expSg_war[0][0], expSg_war[0][1])
                kb.op(ACT, "activation", expSg[0][:], pb[0][:], AF.Exp,
                      bias=0.0, scale=0.125, inc=sa)
                tokEg = sa.v
                pbwar[0].append((sa, tokEg))
                if pend2 is not None:
                    combine_last = emit_combine(pend2)
                pend2 = pend
                pend = {"tokE": tokE, "tokEg": tokEg, "par": par, "j": j,
                        "w": w, "ch": ch, "ro": ro}
            emit_pv(pend)
            emit_combine_a(pend2)
            combine_last = emit_combine(pend2)
            emit_combine_a(pend)
            combine_last = emit_combine(pend)
            pend = None
            pend2 = None

            kb.wait(TEN, sv, combine_last)
            kb.wait(TEN, gq, gq_tok)
            opl = None
            for t in range(NT):
                xc = slice(512 * t, 512 * t + 512)
                for mD in range(DC):
                    bank = (t * DC + mD) % 4
                    war(bank)
                    mm(pb[bank][:], wout_g[0][:, 128 * mD:128 * (mD + 1)],
                       comb[0][:, 512 * t:512 * t + 512], True, False)
                    opl = mm(pb[bank][:], wout_g[1][:, 128 * mD:128 * (mD + 1)],
                             comb[1][:, 512 * t:512 * t + 512], False, True,
                             inc=st)
                    kb.wait(VEC, st, opl)
                    s_, v_ = x_tok[(mD, t)]
                    kb.wait(VEC, s_, v_)
                    scl = bout_sb[:, mD:mD + 1] if g == 0 else 0.0
                    kb.op(VEC, "scalar_tensor_tensor", x[mD][:, xc],
                          pb[bank][:], scl, x[mD][:, xc], ALU.add, ALU.add,
                          inc=sv)
                    x_tok[(mD, t)] = (sv, sv.v)
                    pbwar[bank].append((sv, sv.v))
            oproj_last[g] = opl
            prev_opl[0] = opl
            comb_war[0] = [(st, opl)]
            comb_war[1] = [(st, opl)]
            qk_war = [(st, opl)]
        join_all()
        aes.close()

    if stage >= 1:
        emit_ffn_pass()
    if stage >= 2:
        emit_attention()
    if stage >= 3:
        emit_ffn_pass()

    for c in range(DC):
        for t in range(NT):
            s_, v_ = x_tok[(c, t)]
            kb.wait(SYN, s_, v_)
    for c in range(DC):
        kb.op(SYN, "dma_start", out=d_y[128 * c:128 * (c + 1), :], in_=x[c][:],
              inc=so, inck=16)
    kb.wait(SYN, so, so.v)

    with es, nc.allow_non_contiguous_dma(reason="small one-time gathers (strided halo/kv index patterns)"), nc.Block() as block:
        @block.tensor
        def _(E):
            for f in kb.q["tensor"]:
                f(E)

        @block.vector
        def _(E):
            for f in kb.q["vector"]:
                f(E)

        @block.scalar
        def _(E):
            for f in kb.q["scalar"]:
                f(E)

        @block.gpsimd
        def _(E):
            for f in kb.q["gpsimd"]:
                f(E)

        @block.sync
        def _(E):
            for f in kb.q["sync"]:
                f(E)

    return nc


def make_in_maps(inputs):
    f32 = lambda a: np.ascontiguousarray(np.asarray(a), dtype=np.float32)
    xin = f32(inputs["L0_attn_x"])
    shared = {
        "wqkv": f32(inputs["L0_attn_W_qkv"]),
        "wgate": f32(inputs["L0_attn_W_gate"]),
        "bgate": f32(inputs["L0_attn_b_gate"]).reshape(1, 1),
        "wout": f32(inputs["L0_attn_W_out"]),
        "bout": f32(inputs["L0_attn_b_out"]),
        "gatewT": np.ascontiguousarray(f32(inputs["L0_ffn_gate_w"]).T),
        "gateb": f32(inputs["L0_ffn_gate_b"]).reshape(1, NE),
        "w1": f32(inputs["L0_ffn_w1"]),
        "b1": f32(inputs["L0_ffn_b1"]),
        "w3": f32(inputs["L0_ffn_w3"]),
        "b3": f32(inputs["L0_ffn_b3"]),
        "w2": f32(inputs["L0_ffn_w2"]),
        "sw1": f32(inputs["L0_ffn_sw1"]),
        "sb1": f32(inputs["L0_ffn_sb1"]),
        "sw3": f32(inputs["L0_ffn_sw3"]),
        "sb3": f32(inputs["L0_ffn_sb3"]),
        "sw2": f32(inputs["L0_ffn_sw2"]),
        "b2aug": np.ascontiguousarray(
            np.concatenate([f32(inputs["L0_ffn_b2"]),
                            f32(inputs["L0_ffn_sb2"])[None, :]], axis=0)),
    }
    in_maps = []
    for i in range(8):
        b, h = i // 2, i % 2
        xs = np.ascontiguousarray(xin[b, 2048 * h:2048 * (h + 1), :].T)
        mask = np.zeros(KV, dtype=np.float32)
        if h == 0:
            mask[:256] = MASKVAL
        else:
            mask[2304:] = MASKVAL
        selv = np.array([0.0, 1.0] if h == 0 else [1.0, 0.0], dtype=np.float32)
        sel = np.ascontiguousarray(np.broadcast_to(selv, (128, 2)))
        m = dict(shared)
        m.update({"x_fm": xs, "mask": mask, "sel": sel})
        in_maps.append(m)
    return in_maps


def kernel(**inputs):
    nc = build(STAGE)
    in_maps = make_in_maps(inputs)
    res = run_bass_kernel_spmd(nc, in_maps, core_ids=list(range(8)))
    out = np.empty((4, 4096, D), dtype=np.float32)
    for i in range(8):
        b, h = i // 2, i % 2
        out[b, 2048 * h:2048 * (h + 1), :] = res.results[i]["y_fm"].T
    return out
